# version 14
# baseline (speedup 1.0000x reference)
import contextlib
import numpy as np
import ml_dtypes
import concourse.bass as bass
import concourse.mybir as mybir
from concourse.bass_utils import run_bass_kernel_spmd

F32 = mybir.dt.float32
BF16 = mybir.dt.bfloat16
AF = mybir.ActivationFunctionType
ALU = mybir.AluOpType
AX = mybir.AxisListType
BF = ml_dtypes.bfloat16

D = 2048
H = 5120
NT = 1024
TILES = 8
DC = 16
HCN = 40
EPS = 1e-5
NW = 5120 * 2048
GRP = 6

ENGINES = ("tensor", "vector", "scalar", "gpsimd", "sync")


NLANES = 8


class Prog:

    def __init__(self, nc):
        self.nc = nc
        self.thunks = {e: [] for e in ENGINES}
        self.cnt = {}
        self.lane_idx = {e: 0 for e in ENGINES}
        self.lastwait = {e: {} for e in ENGINES}
        self.state = []
        self.sems = {}

    def step(self, eng, fn, reads=(), writes=(), dma=False, tag=""):
        if dma:
            key = (eng, self.lane_idx[eng] % NLANES)
            self.lane_idx[eng] += 1
        else:
            key = eng
        need = {}

        def bump(f, c):
            if f != key and c > need.get(f, 0):
                need[f] = c

        for (k, lo, hi) in reads:
            for st in self.state:
                if st[0] == k and st[2] > lo and st[1] < hi and st[3] is not None:
                    bump(*st[3])
        for (k, lo, hi) in writes:
            for st in self.state:
                if st[0] == k and st[2] > lo and st[1] < hi:
                    if st[3] is not None:
                        bump(*st[3])
                    for f, c in st[4].items():
                        bump(f, c)
        waits = []
        lw = self.lastwait[eng]
        for f, c in sorted(need.items(), key=str):
            if isinstance(f, str) and f == eng:
                continue
            if lw.get(f, 0) < c:
                waits.append((f, c))
                lw[f] = c
        inc = 16 if dma else 1
        self.cnt[key] = self.cnt.get(key, 0) + inc
        cend = self.cnt[key]
        for (k, lo, hi) in writes:
            keep = []
            for st in self.state:
                if st[0] == k and st[1] >= lo and st[2] <= hi:
                    continue
                keep.append(st)
            self.state = keep
            self.state.append([k, lo, hi, (key, cend), {}])
        for (k, lo, hi) in reads:
            hit = False
            for st in self.state:
                if st[0] == k and st[2] > lo and st[1] < hi:
                    st[4][key] = cend
                    hit = True
            if not hit:
                self.state.append([k, lo, hi, None, {key: cend}])
        self.thunks[eng].append([waits, fn, inc, tag, dma, key])

    def barrier(self, tag="bar"):
        snap_comp = {e: self.cnt.get(e, 0) for e in ENGINES}
        snap_lane = {k: v for k, v in self.cnt.items() if isinstance(k, tuple)}

        def mk(eng):
            def fn(e):
                for f, c in sorted(snap_comp.items()):
                    if f != eng and c:
                        e.wait_ge(self.sems[f], c)
                for f, c in sorted(snap_lane.items(), key=str):
                    e.wait_ge(self.sems[f], c)
                return e.nop()
            return fn
        for eng in ENGINES:
            self.step(eng, mk(eng), tag=f"{tag}_{eng}")
        for st in self.state:
            pass

    def emit(self, es):
        keys = set(self.cnt.keys())
        for e in ENGINES:
            keys.add(e)
        for k in sorted(keys, key=str):
            name = k if isinstance(k, str) else f"{k[0]}_l{k[1]}"
            self.sems[k] = es.enter_context(self.nc.semaphore("sem_" + name))
        targets = {}
        for e in ENGINES:
            for waits, _, _, _, _, _ in self.thunks[e]:
                for f, c in waits:
                    targets.setdefault(f, set()).add(c)
        plans = {}
        for e in ENGINES:
            cum = 0
            pend = 0
            plan = []
            n = len(self.thunks[e])
            last_compute = -1
            for idx, (waits, fn, inc, tag, dma, key) in enumerate(self.thunks[e]):
                if dma:
                    plan.append([waits, fn, 16, tag, key])
                    continue
                cum += inc
                pend += inc
                attach = 0
                if idx == n - 1 or cum in targets.get(e, ()):
                    attach = pend
                    pend = 0
                plan.append([waits, fn, attach, tag, key])
                last_compute = len(plan) - 1
            if pend and last_compute >= 0:
                plan[last_compute][2] += pend
            plans[e] = plan

        block = es.enter_context(self.nc.Block())

        def mk(e):
            def run(eng):
                for waits, fn, attach, tag, key in plans[e]:
                    for f, c in waits:
                        eng.wait_ge(self.sems[f], c)
                    last = fn(eng)
                    if attach:
                        last.then_inc(self.sems[key], attach)
                if e == "sync":
                    for k in sorted(self.cnt.keys(), key=str):
                        if isinstance(k, tuple):
                            eng.wait_ge(self.sems[k], self.cnt[k])
            return run

        for e in ENGINES:
            getattr(block, e)(mk(e))


_off = 0
def _alloc(n):
    global _off
    r = _off
    _off += n
    return r

O_X0 = _alloc(2048)
O_X1 = _alloc(2048)
O_ACT = _alloc(8192)
O_QH = _alloc(8192)
O_VOW = _alloc(8192)
O_WB0 = _alloc(4096)
O_WB1 = _alloc(4096)
O_UB = _alloc(2048)
O_KVP = _alloc(2048)
O_KVB = _alloc(1024)
O_LN0 = _alloc(2048)
O_SCR = _alloc(2048)
O_QFM = _alloc(2048)
O_WAB0 = _alloc(1024)
O_WAB1 = _alloc(1024)
O_ROW = _alloc(2048)
O_DMP = O_SCR + 1024
ASZ = _off
assert ASZ == 52224, ASZ


_CC_HOOK = None


def build(debug=()):
    nc = bass.Bass("TRN2", target_bir_lowering=False, debug=False)

    x = nc.dram_tensor("x", [NT, D], F32, kind="ExternalInput").ap()
    wqk = nc.dram_tensor("wqk", [D, D], F32, kind="ExternalInput").ap()
    wv = nc.dram_tensor("wv", [D, D], F32, kind="ExternalInput").ap()
    wu = nc.dram_tensor("wu", [D, D], F32, kind="ExternalInput").ap()
    wo = nc.dram_tensor("wo", [D, D], F32, kind="ExternalInput").ap()
    vecs = nc.dram_tensor("vecs", [6, D], F32, kind="ExternalInput").ap()
    wfc = nc.dram_tensor("wfc", [H, D], F32, kind="ExternalInput").ap()
    wpj = nc.dram_tensor("wpj", [D, H], F32, kind="ExternalInput").ap()
    ident = nc.dram_tensor("ident", [128, 128], BF16, kind="ExternalInput").ap()
    ones = nc.dram_tensor("ones", [128, 128], F32, kind="ExternalInput").ap()
    y = nc.dram_tensor("y", [NT, D], F32, kind="ExternalOutput").ap()

    u_dram = nc.dram_tensor("u_dram", [TILES, 128, 4, 512], BF16, kind="Internal").ap()
    x1_dram = nc.dram_tensor("x1_dram", [TILES, 128, D], F32, kind="Internal").ap()
    cc_in = nc.dram_tensor("cc_in", [16 * 128, 128], F32, kind="Internal").ap()
    cc_out = nc.dram_tensor("cc_out", [16 * 128, 128], F32, kind="Internal").ap()

    dbg = {}
    for name in debug:
        shp = {"qh": [128, TILES * D], "vh": [128, TILES * D], "kv": [128, 16 * 128],
               "lnx": [128, TILES * D], "asb": [128, TILES * D], "x1": [NT, D],
               "r0": [128, GRP * NT], "fsb": [128, TILES * D], "kvb": [128, 16 * 128],
               "u": [128, TILES * D], "o": [128, TILES * D], "gb": [128, 2 * D],
               "sc": [128, 16], "cols": [128, 160], "ow": [128, TILES * D],
               "ub": [128, 4 * 4096], "ud": [4096, 512], "oln": [128, TILES * D],
               "mv1": [128, 8], "o2": [128, D]}[name]
        dt = BF16 if name in ("qh", "vh", "lnx", "r0", "kvb", "u", "ow", "ub", "ud") else F32
        dbg[name] = nc.dram_tensor("dbg_" + name, shp, dt, kind="ExternalOutput").ap()

    es = contextlib.ExitStack()
    arena = es.enter_context(nc.sbuf_tensor("arena", [128, ASZ], F32))
    af = arena.ap()
    ab = arena.ap().bitcast(BF16)

    id_sb = es.enter_context(nc.sbuf_tensor("id_sb", [128, 128], BF16))
    ones_sb = es.enter_context(nc.sbuf_tensor("ones_sb", [128, 128], F32))
    stats = es.enter_context(nc.sbuf_tensor("stats", [128, 2, 4, 6], F32))
    mv = es.enter_context(nc.sbuf_tensor("mv", [128, 2, 2], F32))
    mvsh = es.enter_context(nc.sbuf_tensor("mvsh", [128, 2, 2], F32))
    rstd = es.enter_context(nc.sbuf_tensor("rstd", [128, 2], F32))
    murstd = es.enter_context(nc.sbuf_tensor("murstd", [128, 2], F32))
    c_eps = es.enter_context(nc.sbuf_tensor("c_eps", [128, 1], F32))
    epsc = es.enter_context(nc.sbuf_tensor("epsc", [128, 1], F32))
    thr = es.enter_context(nc.sbuf_tensor("thr", [128, 4], F32))
    smalls = es.enter_context(nc.sbuf_tensor("smalls", [128, 8], F32))
    wcols = es.enter_context(nc.sbuf_tensor("wcols", [128, 80], F32))
    pcols = es.enter_context(nc.sbuf_tensor("pcols", [128, 80], F32))

    pA = es.enter_context(nc.psum_tensor("pA", [128, 2048], F32))
    pB = es.enter_context(nc.psum_tensor("pB", [128, 1536], F32))
    pT = es.enter_context(nc.psum_tensor("pT", [128, 1024], BF16))
    pT1 = pB.ap()[:, 1024:1536].bitcast(BF16)

    P = Prog(nc)

    def fview(off, n):
        return af[:, off:off + n]

    def bview(off, nbf):
        return ab[:, 2 * off:2 * off + nbf]

    def R(off, n):
        return ("ar", off, off + n)

    def PS(t, lo_col, n_col):
        base = {id(pA): 0, id(pB): 4}[id(t)]
        b0 = base + lo_col // 512
        b1 = base + (lo_col + n_col - 1) // 512 + 1
        return ("ps", b0, b1)

    PT0 = ("ps", 7, 8)
    PT1 = ("ps", 6, 7)

    x_rot = [fview(O_X0, 2048), fview(O_X1, 2048)]
    xr_res = [R(O_X0, 2048), R(O_X1, 2048)]
    act_fm = bview(O_ACT, 16384).rearrange("p (c t n) -> p c t n", c=16, t=8)
    qh = bview(O_QH, 16384).rearrange("p (t f) -> p t f", t=8)
    vh = bview(O_VOW, 16384).rearrange("p (t f) -> p t f", t=8)
    owfm = bview(O_VOW, 16384).rearrange("p (c t n) -> p c t n", c=16, t=8)
    wbuf = [bview(O_WB0, 8192).rearrange("p (c n) -> p c n", c=16),
            bview(O_WB1, 8192).rearrange("p (c n) -> p c n", c=16)]
    wbuf_res = [R(O_WB0, 4096), R(O_WB1, 4096)]
    ubuf = bview(O_UB, 4096).rearrange("p (t n) -> p t n", t=8)
    kv_part = fview(O_KVP, 2048).rearrange("p (h e) -> p h e", h=16)
    kv_bf = bview(O_KVB, 2048).rearrange("p (h e) -> p h e", h=16)
    ln0rot = [bview(O_LN0, 2048), bview(O_LN0 + 1024, 2048)]
    ln0_res = [R(O_LN0, 1024), R(O_LN0 + 1024, 1024)]
    scr = fview(O_SCR, 2048)
    scr_res = R(O_SCR, 2048)
    escr = fview(O_SCR, 512)
    e2 = fview(O_SCR + 512, 512)
    qfm = [bview(O_UB, 2048).rearrange("p (h e) -> p h e", h=16),
           bview(O_UB + 1024, 2048).rearrange("p (h e) -> p h e", h=16)]
    qfm_res = [R(O_UB, 1024), R(O_UB + 1024, 1024)]
    dump = fview(O_DMP, 1024)
    wabs = [fview(O_WAB0, 1024), fview(O_WAB1, 1024)]
    wabs_res = [R(O_WAB0, 1024), R(O_WAB1, 1024)]
    row = af[0:1, O_ROW:O_ROW + 2048]
    row_res = R(O_ROW, 2048)
    brep_full = fview(O_ROW, 2048)
    o_sb = fview(O_ROW, 2048)
    qtmp = [fview(O_QFM, 512), fview(O_QFM + 512, 512)]
    qtmp_res = [R(O_QFM, 512), R(O_QFM + 512, 512)]
    grep = fview(O_LN0, 2048)
    grep_res = R(O_LN0, 2048)
    brep = fview(O_KVP, 2048)
    brep_res = R(O_KVP, 2048)
    urel = [bview(O_QFM, 2048).rearrange("p (q n) -> p q n", q=4),
            bview(O_QFM + 1024, 2048).rearrange("p (q n) -> p q n", q=4)]
    urel_res = [R(O_QFM, 1024), R(O_QFM + 1024, 1024)]
    asb_off = [O_QH, O_QH + 2048, O_QH + 4096, O_QH + 6144, O_UB, O_KVP, O_LN0, O_X0]
    a_sb = [fview(o, 2048) for o in asb_off]
    a_res = [R(o, 2048) for o in asb_off]
    ln2st = [bview(O_QFM, 2048), bview(O_QFM + 1024, 2048)]
    fsb_off = [O_WB0, O_WB0 + 2048, O_WB1, O_WB1 + 2048, O_UB, O_KVP, O_LN0, O_X0]
    f_sb = [fview(o, 2048) for o in fsb_off]
    f_res = [R(o, 2048) for o in fsb_off]
    wpT = bview(O_QH, 12288).rearrange("p (s n) -> p s n", s=GRP)
    wpT_res = R(O_QH, 6144)
    wfT2_list = [bview(O_QH + 6144, 2048).rearrange("p (c n) -> p c n", c=16),
                 bview(O_WAB1, 2048).rearrange("p (c n) -> p c n", c=16)]
    wfT2_res = [R(O_QH + 6144, 1024), R(O_WAB1, 1024)]
    s1L = [bview(O_QH + 7168, 2048), bview(O_ROW, 2048)]
    s1_resL = [R(O_QH + 7168, 1024), R(O_ROW, 1024)]
    r_grp = bview(O_VOW, 6144).rearrange("p (s n) -> p s n", s=GRP)
    r_res = R(O_VOW, 3072)
    s2L = [bview(O_VOW + 3072, 2048), bview(O_ROW + 1024, 2048)]
    s2_resL = [R(O_VOW + 3072, 1024), R(O_ROW + 1024, 1024)]
    q2L = [bview(O_KVB, 2048).rearrange("p (c n) -> p c n", c=16),
           bview(O_X1, 2048).rearrange("p (c n) -> p c n", c=16)]
    q2fL = [bview(O_KVB, 2048), bview(O_X1, 2048)]
    q2_resL = [R(O_KVB, 1024), R(O_X1, 1024)]
    wfcb = [fview(O_VOW + 4096, 2048), fview(O_VOW + 6144, 2048)]
    wfcb_res = [R(O_VOW + 4096, 2048), R(O_VOW + 6144, 2048)]
    wpb = [fview(O_QFM, 2048).rearrange("p (c n) -> p c n", c=16),
           fview(O_WAB0, 2048).rearrange("p (c n) -> p c n", c=16)]
    wpb_flat = [fview(O_QFM, 2048), fview(O_WAB0, 2048)]
    wpb_res = [R(O_QFM, 2048), R(O_WAB0, 2048)]
    x1rel = fview(O_X1, 2048)

    ID = ("id", 0, 1)
    ONES = ("ones", 0, 1)
    SM = lambda name: (name, 0, 1)

    def ln_stats(slot, src_ap, src_res, eps_ap, eps_res, tag=""):
        def vf(v):
            for c in range(4):
                v.bn_stats(stats[:, slot, c, :], src_ap[:, c * 512:(c + 1) * 512])
            v.drain()
            v.bn_aggr(mv[:, slot, :], stats.ap()[:, slot].rearrange("p a b -> p (a b)"))
            v.drain()
            return v.tensor_copy(mvsh[:, slot, :], mv[:, slot, :])
        P.step("vector", vf, reads=[src_res], writes=[SM(f"mv{slot}")], tag="bn" + tag)

        def sf(s):
            s.activation(out=rstd[:, slot:slot + 1], in_=mv[:, slot, 1:2], func=AF.Ln, bias=eps_ap)
            return s.activation(out=rstd[:, slot:slot + 1], in_=rstd[:, slot:slot + 1], func=AF.Exp, scale=c_nh[:])
        P.step("scalar", sf, reads=[SM(f"mv{slot}"), eps_res, SM("c_nh")], writes=[SM(f"rs{slot}")], tag="rstd" + tag)

    def ln_apply(slot, src_ap, src_res, out_ap, out_res, tag=""):
        def vf(v):
            return v.tensor_scalar(out=out_ap, in0=src_ap, scalar1=mv[:, slot, 0:1],
                                   scalar2=rstd[:, slot:slot + 1], op0=ALU.subtract, op1=ALU.mult)
        P.step("vector", vf, reads=[src_res, SM(f"rs{slot}"), SM(f"mv{slot}")], writes=[out_res], tag="lnap" + tag)

    def transpose_batch(src_ap_fn, n, dst_ps, dst_ps_res, reads, tag=""):
        def tf(t):
            last = None
            for k in range(n):
                last = t.transpose(dst_ps[:, k * 128:(k + 1) * 128], src_ap_fn(k), id_sb[:])
            return last
        P.step("tensor", tf, reads=list(reads) + [ID], writes=[dst_ps_res], tag="tr" + tag)

    P.step("sync", lambda e: e.dma_start(out=id_sb[:], in_=ident), writes=[ID], dma=True)
    P.step("sync", lambda e: e.dma_start(out=ones_sb[:], in_=ones), writes=[ONES], dma=True)
    c_nh = es.enter_context(nc.sbuf_tensor("c_nh", [128, 1], F32))

    def init_consts(v):
        v.memset(c_eps[:], EPS)
        return v.memset(c_nh[:], -0.5)
    P.step("vector", init_consts, writes=[SM("c_eps"), SM("c_nh")])
    EPS_RES = SM("c_eps")

    abs_steps = []

    def make_abs_steps():
        idx = [0]
        wfc_r = wfc.rearrange("(a p) d -> p a d", p=128)
        wpj_r = wpj.rearrange("(a p) h -> p a h", p=128)
        chunks = []
        for a in range(40):
            for h2 in range(2):
                chunks.append(("w", wfc_r[:, a, h2 * 1024:(h2 + 1) * 1024], 2 * a + h2))
        for a in range(16):
            for k in range(5):
                chunks.append(("p", wpj_r[:, a, k * 1024:(k + 1) * 1024], 5 * a + k))
        for j, (kind, src, col) in enumerate(chunks):
            rot = j % 2
            cols = wcols if kind == "w" else pcols
            colres = SM("wcols") if kind == "w" else SM("pcols")

            def mk(rot=rot, src=src, cols=cols, col=col, colres=colres, j=j):
                def emit():
                    P.step("sync", lambda e: e.dma_start(out=wabs[rot], in_=src),
                           writes=[wabs_res[rot]], dma=True, tag=f"absdma{j}")
                    def sf(s):
                        return s.activation(out=dump[:, 0:1024], in_=wabs[rot], func=AF.Abs,
                                            accum_out=cols[:, col:col + 1])
                    P.step("scalar", sf, reads=[wabs_res[rot]],
                           writes=[R(O_DMP, 1024), colres], tag=f"absact{j}")
                return emit
            abs_steps.append(mk())
        return abs_steps

    make_abs_steps()
    abs_iter = iter(abs_steps)

    def drip_abs(n):
        for _ in range(n):
            s = next(abs_iter, None)
            if s:
                s()

    for i in range(TILES):
        rot = i % 2
        P.step("sync", lambda e, rot=rot, i=i: e.dma_start(out=x_rot[rot], in_=x[i * 128:(i + 1) * 128, :]),
               reads=[("x", i, i + 1)], writes=[xr_res[rot]], dma=True, tag=f"xdma{i}")
        ln_stats(rot, x_rot[rot], xr_res[rot], c_eps[:], EPS_RES, tag=f"ln0_{i}")
        ln_apply(rot, x_rot[rot], xr_res[rot], ln0rot[rot], ln0_res[rot], tag=f"ln0_{i}")
        for half in range(2):
            ps, psres = (pT, PT0) if half == 0 else (pT1, PT1)
            transpose_batch(lambda k, rot=rot, half=half: ln0rot[rot][:, (half * 8 + k) * 128:(half * 8 + k + 1) * 128],
                            8, ps, psres, [ln0_res[rot]], tag=f"ln0t{i}_{half}")
            P.step("vector", lambda v, ps=ps, i=i, half=half: v.tensor_copy(
                act_fm[:, half * 8:(half + 1) * 8, i, :],
                ps[:, 0:1024].rearrange("p (c n) -> p c n", c=8)),
                reads=[psres], writes=[R(O_ACT + (half * 8) * 512, 4096)], tag=f"ln0c{i}_{half}")
        drip_abs(4)

    def repl_bias(name, bias_idx):
        P.step("sync", lambda e: e.dma_start(out=row, in_=vecs[bias_idx:bias_idx + 1, :]),
               reads=[("vecs", 0, 1)], writes=[row_res], dma=True, tag=f"brow_{name}")

        def bmm(t):
            last = None
            for q in range(4):
                last = t.matmul(pA[:, q * 512:(q + 1) * 512], ones_sb[0:1, :],
                                row[0:1, q * 512:(q + 1) * 512], start=True, stop=True)
            return last
        P.step("tensor", bmm, reads=[row_res, ONES], writes=[PS(pA, 0, 2048)], tag=f"bmm_{name}")
        P.step("vector", lambda v: v.tensor_copy(brep_full, pA[:]),
               reads=[PS(pA, 0, 2048)], writes=[row_res], tag=f"bcp_{name}")

    def wpass(name, wdram, bias_idx, evac, out_writes_fn):
        wr = wdram.rearrange("(c p) n -> p c n", p=128)
        repl_bias(name, bias_idx)
        for q in range(4):
            rot = q % 2
            wres_h = [(wbuf_res[rot][0], wbuf_res[rot][1], wbuf_res[rot][1] + 2048),
                      (wbuf_res[rot][0], wbuf_res[rot][1] + 2048, wbuf_res[rot][2])]
            for h in range(2):
                P.step("gpsimd", lambda e, rot=rot, q=q, h=h, wr=wr: e.dma_start(
                    out=wbuf[rot][:, h * 8:(h + 1) * 8, :], in_=wr[:, h * 8:(h + 1) * 8, q * 512:(q + 1) * 512]),
                    reads=[(name, q, q + 1)], writes=[wres_h[h]], dma=True, tag=f"wdma_{name}{q}_{h}")
            if q == 1 and _CC_HOOK is not None:
                _CC_HOOK()
            for i in range(TILES):
                b = (q * TILES + i) % 4
                psr = PS(pA, b * 512, 512)
                for h in range(2):
                    def mf(t, rot=rot, i=i, b=b, h=h):
                        last = None
                        for fc in range(h * 8, h * 8 + 8):
                            last = t.matmul(pA[:, b * 512:(b + 1) * 512], act_fm[:, fc, i, :],
                                            wbuf[rot][:, fc, :],
                                            start=(fc == 0), stop=(fc == DC - 1))
                        return last
                    P.step("tensor", mf, reads=[wres_h[h], R(O_ACT, 8192)],
                           writes=[psr], tag=f"mm_{name}{q}_{i}_{h}")
                evac(q, i, b, psr)
            drip_abs(10)

    def evac_qk(q, i, b, psr):
        rot2 = i % 2
        P.step("vector", lambda v, b=b, q=q, rot2=rot2: v.tensor_add(
            qtmp[rot2], pA[:, b * 512:(b + 1) * 512], brep_full[:, q * 512:(q + 1) * 512]),
            reads=[psr, row_res], writes=[qtmp_res[rot2]], tag=f"badd{q}_{i}")
        P.step("scalar", lambda s, rot2=rot2: s.activation(out=escr, in_=qtmp[rot2], func=AF.Exp),
               reads=[qtmp_res[rot2]], writes=[R(O_SCR, 512)], tag=f"eluexp{q}_{i}")

        def vf(v, q=q, i=i, rot2=rot2):
            v.tensor_scalar(out=e2, in0=escr, scalar1=1.0, scalar2=1.0, op0=ALU.min, op1=ALU.subtract)
            return v.scalar_tensor_tensor(out=qh[:, i, q * 512:(q + 1) * 512], in0=qtmp[rot2],
                                          scalar=0.0, in1=e2, op0=ALU.max, op1=ALU.add)
        P.step("vector", vf, reads=[R(O_SCR, 512), R(O_SCR + 512, 512), qtmp_res[rot2]],
               writes=[R(O_QH + i * 1024 + q * 256, 256), R(O_SCR + 512, 512)], tag=f"eluf{q}_{i}")

    wpass("wqk", wqk, 0, evac_qk, None)

    def evac_v(q, i, b, psr):
        rot2 = i % 2
        P.step("vector", lambda v, b=b, q=q, rot2=rot2: v.tensor_add(
            qtmp[rot2], pA[:, b * 512:(b + 1) * 512], brep_full[:, q * 512:(q + 1) * 512]),
            reads=[psr, row_res], writes=[qtmp_res[rot2]], tag=f"baddv{q}_{i}")
        P.step("scalar", lambda s, q=q, i=i, rot2=rot2: s.activation(
            out=vh[:, i, q * 512:(q + 1) * 512], in_=qtmp[rot2], func=AF.Silu),
            reads=[qtmp_res[rot2]], writes=[R(O_VOW + i * 1024 + q * 256, 256)], tag=f"silv{q}_{i}")

    wpass("wv", wv, 1, evac_v, None)

    slot_list = [(pA, 0), (pA, 512), (pA, 1024), (pA, 1536), (pB, 0), (pB, 512), (pB, 1024)]
    heads = list(range(16))
    hi = 0
    while hi < 16:
        rnd = heads[hi:hi + 7]
        for k, h in enumerate(rnd):
            t_, off = slot_list[k]
            psr = PS(t_, off, 128)

            def kf(t, t_=t_, off=off, h=h):
                last = None
                for i in range(TILES):
                    last = t.matmul(t_[:, off:off + 128], qh[:, i, h * 128:(h + 1) * 128],
                                    vh[:, i, h * 128:(h + 1) * 128], start=(i == 0), stop=(i == TILES - 1))
                return last
            P.step("tensor", kf, reads=[R(O_QH, 8192), R(O_VOW, 8192)], writes=[psr], tag=f"kv{h}")
        for k, h in enumerate(rnd):
            t_, off = slot_list[k]
            psr = PS(t_, off, 128)
            P.step("vector", lambda v, t_=t_, off=off, h=h: v.tensor_copy(kv_part[:, h, :], t_[:, off:off + 128]),
                   reads=[psr], writes=[R(O_KVP + h * 128, 128)], tag=f"kvc{h}")
        hi += 7
    P.step("sync", lambda e: e.dma_start(out=cc_in.rearrange("(h d) e -> d h e", d=128), in_=kv_part),
           reads=[R(O_KVP, 2048)], writes=[("cc_in", 0, 1)], dma=True, tag="cc_w")
    cc_sem = es.enter_context(nc.semaphore("cc_sem"))

    def emit_collective():
        def ccf(e):
            e.collective_compute(
                "AllReduce", ALU.add, replica_groups=[[0, 1, 2, 3], [4, 5, 6, 7]],
                ins=[cc_in.opt()], outs=[cc_out.opt()]).then_inc(cc_sem)
            e.wait_ge(cc_sem, 1)
            return e.nop()
        P.step("gpsimd", ccf, reads=[("cc_in", 0, 1)], writes=[("cc_out", 0, 1)], tag="ccar")
        P.step("gpsimd", lambda e: e.dma_start(out=kv_bf,
                                               in_=cc_out.rearrange("(h d) e -> d h e", d=128)),
               reads=[("cc_out", 0, 1)], writes=[R(O_KVB, 1024)], dma=True, tag="cc_r")
        if "kvb" in dbg:
            P.step("sync", lambda e: e.dma_start(out=dbg["kvb"], in_=kv_bf.rearrange("p h e -> p (h e)")),
                   reads=[R(O_KVB, 1024)], dma=True)

    def evac_u(q, i, b, psr):
        rot2 = i % 2
        P.step("vector", lambda v, b=b, q=q, rot2=rot2: v.tensor_add(
            qtmp[rot2], pA[:, b * 512:(b + 1) * 512], brep_full[:, q * 512:(q + 1) * 512]),
            reads=[psr, row_res], writes=[qtmp_res[rot2]], tag=f"baddu{q}_{i}")
        P.step("scalar", lambda s, i=i, rot2=rot2: s.activation(
            out=ubuf[:, i, :], in_=qtmp[rot2], func=AF.Silu),
            reads=[qtmp_res[rot2]], writes=[R(O_UB + i * 256, 256)], tag=f"silu{q}_{i}")
        if i == TILES - 1:
            P.step("sync", lambda e, q=q: e.dma_start(
                out=u_dram[:, :, q, :].rearrange("t p n -> p t n"), in_=ubuf[:]),
                reads=[R(O_UB, 2048)], writes=[("u_dram", q, q + 1)], dma=True, tag=f"udma{q}")
            if "ub" in dbg:
                P.step("sync", lambda e, q=q: e.dma_start(
                    out=dbg["ub"][:, q * 4096:(q + 1) * 4096],
                    in_=ubuf.rearrange("p t n -> p (t n)")),
                    reads=[R(O_UB, 2048)], dma=True)

    global _CC_HOOK
    _CC_HOOK = emit_collective
    wpass("wu", wu, 2, evac_u, None)
    _CC_HOOK = None

    drip_abs(200)

    if "qh" in dbg:
        P.step("sync", lambda e: e.dma_start(out=dbg["qh"], in_=qh.rearrange("p t f -> p (t f)")),
               reads=[R(O_QH, 8192)], dma=True)
    if "vh" in dbg:
        P.step("sync", lambda e: e.dma_start(out=dbg["vh"], in_=vh.rearrange("p t f -> p (t f)")),
               reads=[R(O_VOW, 8192)], dma=True)
    if "kv" in dbg:
        P.step("sync", lambda e: e.dma_start(out=dbg["kv"], in_=kv_part.rearrange("p h e -> p (h e)")),
               reads=[R(O_KVP, 2048)], dma=True)

    for j, (dst, dstres) in enumerate([(grep, grep_res), (brep, brep_res)]):
        P.step("sync", lambda e, j=j: e.dma_start(out=row, in_=vecs[4 + j:5 + j, :]),
               reads=[("vecs", 0, 1)], writes=[row_res], dma=True, tag=f"gbrow{j}")

        def gf(t, j=j):
            last = None
            for q in range(4):
                last = t.matmul(pA[:, q * 512:(q + 1) * 512], ones_sb[0:1, :],
                                row[0:1, q * 512:(q + 1) * 512], start=True, stop=True)
            return last
        P.step("tensor", gf, reads=[row_res, ONES], writes=[PS(pA, 0, 2048)], tag=f"gbmm{j}")
        P.step("vector", lambda v, dst=dst: v.tensor_copy(dst, pA[:]),
               reads=[PS(pA, 0, 2048)], writes=[dstres], tag=f"gbc{j}")

    if "ud" in dbg:
        P.step("gpsimd", lambda e: e.dma_start(out=dbg["ud"], in_=u_dram.rearrange("t p q n -> (t p q) n")),
               reads=[("u_dram", 0, 4)], dma=True)
    if "gb" in dbg:
        P.step("sync", lambda e: e.dma_start(out=dbg["gb"][:, 0:D], in_=grep), reads=[grep_res], dma=True)
        P.step("sync", lambda e: e.dma_start(out=dbg["gb"][:, D:2 * D], in_=brep), reads=[brep_res], dma=True)
    for i in range(TILES):
        rot = i % 2
        P.step("gpsimd", lambda e, rot=rot, i=i: e.dma_start(
            out=urel[rot], in_=u_dram[i, :, :, :]),
            reads=[("u_dram", 0, 4)], writes=[urel_res[rot]], dma=True, tag=f"urel{i}")
        for half in range(2):
            ps, psres = (pT, PT0) if half == 0 else (pT1, PT1)
            transpose_batch(lambda k, i=i, half=half: qh[:, i, (half * 8 + k) * 128:(half * 8 + k + 1) * 128],
                            8, ps, psres, [R(O_QH + i * 1024, 1024)], tag=f"qt{i}_{half}")
            P.step("vector", lambda v, ps=ps, rot=rot, half=half: v.tensor_copy(
                qfm[rot][:, half * 8:(half + 1) * 8, :],
                ps[:, 0:1024].rearrange("p (c n) -> p c n", c=8)),
                reads=[psres], writes=[qfm_res[rot]], tag=f"qtc{i}_{half}")
        if "u" in dbg:
            P.step("sync", lambda e, rot=rot, i=i: e.dma_start(
                out=dbg["u"][:, i * D:(i + 1) * D], in_=urel[rot].rearrange("p q n -> p (q n)")),
                reads=[urel_res[rot]], dma=True)
        def of(t, rot=rot):
            last = None
            for h in range(16):
                last = t.matmul(pA[:, h * 128:(h + 1) * 128], qfm[rot][:, h, :], kv_bf[:, h, :],
                                start=(h % 4 == 0), stop=(h % 4 == 3))
            return last
        P.step("tensor", of, reads=[qfm_res[rot], R(O_KVB, 1024)], writes=[PS(pA, 0, 2048)], tag=f"omm{i}")
        osb_i = x_rot[rot]
        osb_res = xr_res[rot]
        P.step("scalar", lambda s, osb_i=osb_i: s.copy(out=osb_i, in_=pA[:]),
               reads=[PS(pA, 0, 2048)], writes=[osb_res], tag=f"oevac{i}")
        if "o" in dbg:
            P.step("sync", lambda e, i=i, osb_i=osb_i: e.dma_start(out=dbg["o"][:, i * D:(i + 1) * D], in_=osb_i),
                   reads=[osb_res], dma=True)
        ln_stats(rot, osb_i, osb_res, c_eps[:], EPS_RES, tag=f"ln1_{i}")
        ln_apply(rot, osb_i, osb_res, osb_i, osb_res, tag=f"ln1_{i}")
        if "oln" in dbg:
            P.step("sync", lambda e, i=i, osb_i=osb_i: e.dma_start(out=dbg["oln"][:, i * D:(i + 1) * D], in_=osb_i),
                   reads=[osb_res], dma=True)
        owb_i = ab[:, 2 * (O_SCR + rot * 1024):2 * (O_SCR + rot * 1024) + 2048]
        owb_res = R(O_SCR + rot * 1024, 1024)

        P.step("vector", lambda v, osb_i=osb_i: v.tensor_mul(osb_i, osb_i, grep),
               reads=[osb_res, grep_res], writes=[osb_res], tag=f"gmul{i}")
        P.step("gpsimd", lambda g, osb_i=osb_i: g.tensor_add(osb_i, osb_i, brep),
               reads=[osb_res, brep_res], writes=[osb_res], tag=f"badd3_{i}")
        P.step("vector", lambda v, rot=rot, osb_i=osb_i, owb_i=owb_i: v.tensor_mul(
            owb_i, osb_i, urel[rot].rearrange("p q n -> p (q n)")),
            reads=[osb_res, urel_res[rot]], writes=[owb_res], tag=f"umul{i}")
        if "ow" in dbg:
            P.step("sync", lambda e, i=i, owb_i=owb_i: e.dma_start(out=dbg["ow"][:, i * D:(i + 1) * D],
                                                                   in_=owb_i),
                   reads=[owb_res], dma=True)
        for half in range(2):
            ps, psres = (pT, PT0) if half == 0 else (pT1, PT1)
            transpose_batch(lambda k, half=half, rot=rot: ab[:, 2 * (O_SCR + rot * 1024) + (half * 8 + k) * 128:
                                                             2 * (O_SCR + rot * 1024) + (half * 8 + k + 1) * 128],
                            8, ps, psres, [owb_res], tag=f"owt{i}_{half}")
            P.step("vector", lambda v, ps=ps, i=i, half=half: v.tensor_copy(
                owfm[:, half * 8:(half + 1) * 8, i, :],
                ps[:, 0:1024].rearrange("p (c n) -> p c n", c=8)),
                reads=[psres], writes=[R(O_VOW + (half * 8) * 512, 4096)], tag=f"owc{i}_{half}")
        drip_abs(4)

    repl_bias("wo", 3)
    wor = wo.rearrange("(c p) n -> p c n", p=128)
    for q in range(4):
        rot = q % 2
        P.step("gpsimd", lambda e, rot=rot, q=q: e.dma_start(
            out=wbuf[rot][:], in_=wor[:, :, q * 512:(q + 1) * 512]),
            reads=[("wo", q, q + 1)], writes=[wbuf_res[rot]], dma=True, tag=f"wodma{q}")
        for i in range(TILES):
            b = (q * TILES + i) % 4
            psr = PS(pA, b * 512, 512)

            def mo(t, rot=rot, q=q, i=i, b=b):
                last = None
                for fc in range(DC):
                    last = t.matmul(pA[:, b * 512:(b + 1) * 512], owfm[:, fc, i, :], wbuf[rot][:, fc, :],
                                    start=(fc == 0), stop=(fc == DC - 1))
                return last
            P.step("tensor", mo, reads=[wbuf_res[rot], R(O_VOW, 8192)],
                   writes=[psr], tag=f"womm{q}_{i}")
            P.step("vector", lambda v, b=b, q=q, i=i: v.tensor_add(
                a_sb[i][:, q * 512:(q + 1) * 512], pA[:, b * 512:(b + 1) * 512],
                brep_full[:, q * 512:(q + 1) * 512]),
                reads=[psr, row_res], writes=[(a_res[i][0], a_res[i][1] + q * 512, a_res[i][1] + (q + 1) * 512)],
                tag=f"woev{q}_{i}")
        drip_abs(2)

    if "asb" in dbg:
        for i in range(TILES):
            P.step("sync", lambda e, i=i: e.dma_start(out=dbg["asb"][:, i * 2048:(i + 1) * 2048], in_=a_sb[i]),
                   reads=[a_res[i]], dma=True)

    for i in range(TILES):
        rot = i % 2
        P.step("sync", lambda e, i=i: e.dma_start(out=x1rel, in_=x[i * 128:(i + 1) * 128, :]),
               reads=[("x", i, i + 1)], writes=[R(O_X1, 2048)], dma=True, tag=f"xrel{i}")
        ln_stats(rot, a_sb[i], a_res[i], c_eps[:], EPS_RES, tag=f"lna_{i}")
        ln_apply(rot, a_sb[i], a_res[i], scr, scr_res, tag=f"lna_{i}")
        P.step("vector", lambda v: v.tensor_add(x1rel, scr, x1rel),
               reads=[scr_res, R(O_X1, 2048)], writes=[R(O_X1, 2048)], tag=f"x1add{i}")
        P.step("sync", lambda e, i=i: e.dma_start(out=x1_dram[i, :, :], in_=x1rel),
               reads=[R(O_X1, 2048)], writes=[("x1d", i, i + 1)], dma=True, tag=f"x1w{i}")
        ln_stats(rot, x1rel, R(O_X1, 2048), c_eps[:], EPS_RES, tag=f"ln2_{i}")
        ln_apply(rot, x1rel, R(O_X1, 2048), ln2st[rot], qfm_res[rot], tag=f"ln2_{i}")
        for half in range(2):
            ps, psres = (pT, PT0) if half == 0 else (pT1, PT1)
            transpose_batch(lambda k, rot=rot, half=half: ln2st[rot][:, (half * 8 + k) * 128:(half * 8 + k + 1) * 128],
                            8, ps, psres, [qfm_res[rot]], tag=f"ln2t{i}_{half}")
            P.step("vector", lambda v, ps=ps, i=i, half=half: v.tensor_copy(
                act_fm[:, half * 8:(half + 1) * 8, i, :],
                ps[:, 0:1024].rearrange("p (c n) -> p c n", c=8)),
                reads=[psres], writes=[R(O_ACT + (half * 8) * 512, 4096)], tag=f"ln2c{i}_{half}")
        drip_abs(2)

    if "x1" in dbg:
        for i in range(TILES):
            P.step("gpsimd", lambda e, i=i: e.dma_start(out=dbg["x1"][i * 128:(i + 1) * 128, :], in_=x1_dram[i, :, :]),
                   reads=[("x1d", i, i + 1)], dma=True)

    drip_abs(200)

    P.barrier("barB2C")

    def redsum(v):
        v.tensor_reduce(out=smalls[:, 0:1], in_=wcols[:], axis=AX.X, op=ALU.add)
        v.drain()
        return v.tensor_reduce(out=smalls[:, 1:2], in_=pcols[:], axis=AX.X, op=ALU.add)
    P.step("vector", redsum, reads=[SM("wcols"), SM("pcols")], writes=[SM("sums")], tag="redsum")

    def totmm(t):
        t.matmul(pB[:, 1024:1025], ones_sb[:], smalls[:, 0:1], start=True, stop=True)
        return t.matmul(pB[:, 1028:1029], ones_sb[:], smalls[:, 1:2], start=True, stop=True)
    P.step("tensor", totmm, reads=[SM("sums"), ONES], writes=[PS(pB, 1024, 512)], tag="totmm")

    INV_N = float(np.float32(1.0 / NW))

    def chain(v):
        d = v.drain
        v.tensor_scalar(out=smalls[:, 2:3], in0=pB[:, 1024:1025], scalar1=INV_N, scalar2=EPS,
                        op0=ALU.mult, op1=ALU.max); d()
        v.tensor_scalar(out=smalls[:, 3:4], in0=pB[:, 1028:1029], scalar1=INV_N, scalar2=EPS,
                        op0=ALU.mult, op1=ALU.max); d()
        v.tensor_scalar(out=thr[:, 0:1], in0=smalls[:, 2:3], scalar1=0.7, scalar2=None, op0=ALU.mult); d()
        v.tensor_scalar(out=thr[:, 1:2], in0=thr[:, 0:1], scalar1=-1.0, scalar2=None, op0=ALU.mult); d()
        v.tensor_scalar(out=thr[:, 2:3], in0=smalls[:, 3:4], scalar1=0.7, scalar2=None, op0=ALU.mult); d()
        v.tensor_scalar(out=thr[:, 3:4], in0=thr[:, 2:3], scalar1=-1.0, scalar2=None, op0=ALU.mult); d()
        v.tensor_mul(smalls[:, 4:5], smalls[:, 2:3], smalls[:, 2:3]); d()
        v.tensor_mul(smalls[:, 5:6], smalls[:, 4:5], smalls[:, 3:4]); d()
        v.tensor_scalar(out=smalls[:, 6:7], in0=smalls[:, 5:6], scalar1=0.125, scalar2=None, op0=ALU.mult); d()
        v.reciprocal(smalls[:, 7:8], smalls[:, 6:7]); d()
        v.tensor_mul(smalls[:, 0:1], smalls[:, 7:8], smalls[:, 7:8]); d()
        v.tensor_scalar(out=epsc[:], in0=smalls[:, 0:1], scalar1=EPS, scalar2=None, op0=ALU.mult); d()
        return v.tensor_copy(smalls[:, 1:2], epsc[:])
    P.step("vector", chain, reads=[PS(pB, 1024, 512)], writes=[SM("thr"), SM("epsc"), SM("sums")], tag="chain")

    if "sc" in dbg:
        P.step("sync", lambda e: e.dma_start(out=dbg["sc"][:, 0:8], in_=smalls[:]),
               reads=[SM("thr"), SM("epsc"), SM("sums")], dma=True, tag="scdump1")
        P.step("sync", lambda e: e.dma_start(out=dbg["sc"][:, 8:12], in_=thr[:]),
               reads=[SM("thr")], dma=True, tag="scdump2")
    if "cols" in dbg:
        P.step("sync", lambda e: e.dma_start(out=dbg["cols"][:, 0:80], in_=wcols[:]),
               reads=[SM("wcols")], dma=True, tag="coldump1")
        P.step("sync", lambda e: e.dma_start(out=dbg["cols"][:, 80:160], in_=pcols[:]),
               reads=[SM("pcols")], dma=True, tag="coldump2")

    wfc_rows = wfc.rearrange("(a p) d -> p a d", p=128)
    wpj_cols = wpj.rearrange("(c p) h -> p c h", p=128)

    groups = [list(range(g, min(g + GRP, HCN))) for g in range(0, HCN, GRP)]
    for gi, grp in enumerate(groups):
        for sj, hc in enumerate(grp):
            rot = hc % 2
            s1, s1_res = s1L[rot], s1_resL[rot]
            s2, s2_res = s2L[rot], s2_resL[rot]
            q2wp, q2wp_flat, q2wp_res = q2L[rot], q2fL[rot], q2_resL[rot]
            P.step("sync", lambda e, rot=rot, hc=hc: e.dma_start(out=wfcb[rot], in_=wfc_rows[:, hc, :]),
                   reads=[("wfc", hc, hc + 1)], writes=[wfcb_res[rot]], dma=True, tag=f"fcd{hc}")
            P.step("sync", lambda e, rot=rot, hc=hc: e.dma_start(
                out=wpb[rot], in_=wpj_cols[:, :, hc * 128:(hc + 1) * 128]),
                reads=[("wpj", hc, hc + 1)], writes=[wpb_res[rot]], dma=True, tag=f"wpd{hc}")
            def tf1(s, rot=rot, s1=s1, s2=s2):
                s.activation(out=s1, in_=wfcb[rot], func=AF.Sign, bias=thr[:, 1:2])
                return s.activation(out=s2, in_=wfcb[rot], func=AF.Sign, bias=thr[:, 0:1])
            P.step("scalar", tf1, reads=[wfcb_res[rot], SM("thr")], writes=[s1_res, s2_res], tag=f"sgf{hc}")
            P.step("vector", lambda v, s1=s1, s2=s2: v.tensor_add(s1, s1, s2),
                   reads=[s1_res, s2_res], writes=[s1_res], tag=f"qf{hc}")
            def tf2(s, rot=rot, q2wp_flat=q2wp_flat, s2=s2):
                s.activation(out=q2wp_flat, in_=wpb_flat[rot], func=AF.Sign, bias=thr[:, 3:4])
                return s.activation(out=s2, in_=wpb_flat[rot], func=AF.Sign, bias=thr[:, 2:3])
            P.step("scalar", tf2, reads=[wpb_res[rot], SM("thr")], writes=[q2wp_res, s2_res], tag=f"sgp{hc}")
            P.step("vector", lambda v, q2wp_flat=q2wp_flat, s2=s2: v.tensor_add(q2wp_flat, q2wp_flat, s2),
                   reads=[q2wp_res, s2_res], writes=[q2wp_res], tag=f"qp{hc}")
            wfT = wfT2_list[hc % 2]
            wfT_res = wfT2_res[hc % 2]
            for half in range(2):
                ps, psres = (pT, PT0) if half == 0 else (pT1, PT1)
                transpose_batch(lambda k, half=half, s1=s1: s1[:, (half * 8 + k) * 128:(half * 8 + k + 1) * 128],
                                8, ps, psres, [s1_res], tag=f"ft{hc}_{half}")
                P.step("vector", lambda v, ps=ps, half=half, wfT=wfT: v.tensor_copy(
                    wfT[:, half * 8:(half + 1) * 8, :],
                    ps[:, 0:1024].rearrange("p (c n) -> p c n", c=8)),
                    reads=[psres], writes=[wfT_res], tag=f"ftc{hc}_{half}")
            for half in range(2):
                ps, psres = (pT, PT0) if half == 0 else (pT1, PT1)
                transpose_batch(lambda k, half=half, q2wp=q2wp: q2wp[:, half * 8 + k, :],
                                8, ps, psres, [q2wp_res], tag=f"pt{hc}_{half}")
                P.step("vector", lambda v, ps=ps, sj=sj, half=half: v.tensor_copy(
                    wpT[:, sj, half * 1024:(half + 1) * 1024], ps[:, 0:1024]),
                    reads=[psres], writes=[(wpT_res[0], wpT_res[1] + sj * 1024, wpT_res[1] + (sj + 1) * 1024)],
                    tag=f"ptc{hc}_{half}")
            for half in range(2):
                psr = PS(pB, half * 512, 512)

                def rf(t, half=half, wfT=wfT):
                    last = None
                    for dc in range(DC):
                        last = t.matmul(pB[:, half * 512:(half + 1) * 512], wfT[:, dc, :],
                                        act_fm[:, dc, half * 4:(half + 1) * 4, :].rearrange("p t n -> p (t n)"),
                                        start=(dc == 0), stop=(dc == DC - 1))
                    return last
                P.step("tensor", rf, reads=[wfT_res, R(O_ACT, 8192)], writes=[psr], tag=f"rmm{hc}_{half}")
                P.step("vector", lambda v, half=half: v.tensor_scalar(
                    out=scr[:, half * 512:(half + 1) * 512], in0=pB[:, half * 512:(half + 1) * 512],
                    scalar1=0.0, scalar2=None, op0=ALU.max),
                    reads=[psr], writes=[(("ar", O_SCR + half * 512, O_SCR + (half + 1) * 512))], tag=f"rr{hc}_{half}")
                P.step("vector", lambda v, sj=sj, half=half: v.tensor_mul(
                    r_grp[:, sj, half * 512:(half + 1) * 512], scr[:, half * 512:(half + 1) * 512],
                    scr[:, half * 512:(half + 1) * 512]),
                    reads=[("ar", O_SCR + half * 512, O_SCR + (half + 1) * 512), psr],
                    writes=[(r_res[0], r_res[1] + sj * 512, r_res[1] + (sj + 1) * 512)], tag=f"rsq{hc}_{half}")
        if "r0" in dbg and gi == 0:
            P.step("sync", lambda e: e.dma_start(out=dbg["r0"], in_=r_grp.rearrange("p s n -> p (s n)")),
                   reads=[r_res], dma=True)
        for i in range(TILES):
            for cg in range(4):
                psr = PS(pA, cg * 512, 512)

                def ff(t, i=i, cg=cg, grp=grp):
                    last = None
                    for sj in range(len(grp)):
                        last = t.matmul(pA[:, cg * 512:(cg + 1) * 512], r_grp[:, sj, i * 128:(i + 1) * 128],
                                        wpT[:, sj, cg * 512:(cg + 1) * 512],
                                        start=(sj == 0), stop=(sj == len(grp) - 1))
                    return last
                P.step("tensor", ff, reads=[r_res, wpT_res], writes=[psr], tag=f"fmm{gi}_{i}_{cg}")
            if gi == 0:
                P.step("vector", lambda v, i=i: v.tensor_copy(f_sb[i], pA[:]),
                       reads=[PS(pA, 0, 2048)], writes=[f_res[i]], tag=f"fc{gi}_{i}")
            else:
                P.step("vector", lambda v, i=i: v.tensor_add(f_sb[i], f_sb[i], pA[:]),
                       reads=[PS(pA, 0, 2048), f_res[i]], writes=[f_res[i]], tag=f"fa{gi}_{i}")

    if "fsb" in dbg:
        for i in range(TILES):
            P.step("sync", lambda e, i=i: e.dma_start(out=dbg["fsb"][:, i * 2048:(i + 1) * 2048], in_=f_sb[i]),
                   reads=[f_res[i]], dma=True)

    for i in range(TILES):
        rot = i % 2
        P.step("gpsimd", lambda e, i=i: e.dma_start(out=x1rel, in_=x1_dram[i, :, :]),
               reads=[("x1d", i, i + 1)], writes=[R(O_X1, 2048)], dma=True, tag=f"x1r{i}")
        ln_stats(rot, f_sb[i], f_res[i], epsc[:], SM("epsc"), tag=f"lnf_{i}")
        ln_apply(rot, f_sb[i], f_res[i], scr, scr_res, tag=f"lnf_{i}")
        P.step("vector", lambda v: v.tensor_add(scr, scr, x1rel),
               reads=[scr_res, R(O_X1, 2048)], writes=[scr_res], tag=f"oadd{i}")
        P.step("sync", lambda e, i=i: e.dma_start(out=y[i * 128:(i + 1) * 128, :], in_=scr),
               reads=[scr_res], writes=[("y", i, i + 1)], dma=True, tag=f"yw{i}")

    P.emit(es)
    es.close()
    return nc


_NC_CACHE = {}


def _get_nc(debug=()):
    key = tuple(sorted(debug))
    if key not in _NC_CACHE:
        _NC_CACHE[key] = build(debug=key)
    return _NC_CACHE[key]


def kernel(L0_attn_x, L0_attn_Wqk, L0_attn_bqk, L0_attn_Wv, L0_attn_bv,
           L0_attn_Wu, L0_attn_bu, L0_attn_Wo, L0_attn_bo,
           L0_attn_ln_g, L0_attn_ln_b, L0_ffn_w_fc, L0_ffn_w_proj,
           _debug=(), _results_hook=None):
    f32 = np.float32
    xfull = np.ascontiguousarray(np.asarray(L0_attn_x, f32).reshape(8 * NT, D))
    vecs = np.ascontiguousarray(np.stack([
        np.asarray(L0_attn_bqk, f32), np.asarray(L0_attn_bv, f32),
        np.asarray(L0_attn_bu, f32), np.asarray(L0_attn_bo, f32),
        np.asarray(L0_attn_ln_g, f32), np.asarray(L0_attn_ln_b, f32)], axis=0))
    base = dict(
        wqk=np.ascontiguousarray(np.asarray(L0_attn_Wqk, f32)),
        wv=np.ascontiguousarray(np.asarray(L0_attn_Wv, f32)),
        wu=np.ascontiguousarray(np.asarray(L0_attn_Wu, f32)),
        wo=np.ascontiguousarray(np.asarray(L0_attn_Wo, f32)),
        vecs=vecs,
        wfc=np.ascontiguousarray(np.asarray(L0_ffn_w_fc, f32)),
        wpj=np.ascontiguousarray(np.asarray(L0_ffn_w_proj, f32)),
        ident=np.eye(128, dtype=BF),
        ones=np.ones((128, 128), dtype=f32),
    )
    nc = _get_nc(_debug)
    in_maps = [dict(base, x=np.ascontiguousarray(xfull[i * NT:(i + 1) * NT])) for i in range(8)]
    res = run_bass_kernel_spmd(nc, in_maps, core_ids=list(range(8)))
    if _results_hook is not None:
        _results_hook(res)
    out = np.concatenate([np.asarray(res.results[i]["y"]) for i in range(8)], axis=0)
    return out.reshape(2, 4096, D).astype(np.float32)


# version 15
# speedup vs baseline: 1.0021x; 1.0021x over previous
import contextlib
import numpy as np
import ml_dtypes
import concourse.bass as bass
import concourse.mybir as mybir
from concourse.bass_utils import run_bass_kernel_spmd

F32 = mybir.dt.float32
BF16 = mybir.dt.bfloat16
AF = mybir.ActivationFunctionType
ALU = mybir.AluOpType
AX = mybir.AxisListType
BF = ml_dtypes.bfloat16

D = 2048
H = 5120
NT = 1024
TILES = 8
DC = 16
HCN = 40
EPS = 1e-5
NW = 5120 * 2048
GRP = 6

ENGINES = ("tensor", "vector", "scalar", "gpsimd", "sync")


NLANES = 8


class Prog:

    def __init__(self, nc):
        self.nc = nc
        self.thunks = {e: [] for e in ENGINES}
        self.cnt = {}
        self.lane_idx = {e: 0 for e in ENGINES}
        self.lastwait = {e: {} for e in ENGINES}
        self.state = []
        self.sems = {}

    def step(self, eng, fn, reads=(), writes=(), dma=False, tag=""):
        if dma:
            key = (eng, self.lane_idx[eng] % NLANES)
            self.lane_idx[eng] += 1
        else:
            key = eng
        need = {}

        def bump(f, c):
            if f != key and c > need.get(f, 0):
                need[f] = c

        for (k, lo, hi) in reads:
            for st in self.state:
                if st[0] == k and st[2] > lo and st[1] < hi and st[3] is not None:
                    bump(*st[3])
        for (k, lo, hi) in writes:
            for st in self.state:
                if st[0] == k and st[2] > lo and st[1] < hi:
                    if st[3] is not None:
                        bump(*st[3])
                    for f, c in st[4].items():
                        bump(f, c)
        waits = []
        lw = self.lastwait[eng]
        for f, c in sorted(need.items(), key=str):
            if isinstance(f, str) and f == eng:
                continue
            if lw.get(f, 0) < c:
                waits.append((f, c))
                lw[f] = c
        inc = 16 if dma else 1
        self.cnt[key] = self.cnt.get(key, 0) + inc
        cend = self.cnt[key]
        for (k, lo, hi) in writes:
            keep = []
            for st in self.state:
                if st[0] == k and st[1] >= lo and st[2] <= hi:
                    continue
                keep.append(st)
            self.state = keep
            self.state.append([k, lo, hi, (key, cend), {}])
        for (k, lo, hi) in reads:
            hit = False
            for st in self.state:
                if st[0] == k and st[2] > lo and st[1] < hi:
                    st[4][key] = cend
                    hit = True
            if not hit:
                self.state.append([k, lo, hi, None, {key: cend}])
        self.thunks[eng].append([waits, fn, inc, tag, dma, key])

    def barrier(self, tag="bar"):
        snap_comp = {e: self.cnt.get(e, 0) for e in ENGINES}
        snap_lane = {k: v for k, v in self.cnt.items() if isinstance(k, tuple)}

        def mk(eng):
            def fn(e):
                for f, c in sorted(snap_comp.items()):
                    if f != eng and c:
                        e.wait_ge(self.sems[f], c)
                for f, c in sorted(snap_lane.items(), key=str):
                    e.wait_ge(self.sems[f], c)
                return e.nop()
            return fn
        for eng in ENGINES:
            self.step(eng, mk(eng), tag=f"{tag}_{eng}")
        for st in self.state:
            pass

    def emit(self, es):
        keys = set(self.cnt.keys())
        for e in ENGINES:
            keys.add(e)
        for k in sorted(keys, key=str):
            name = k if isinstance(k, str) else f"{k[0]}_l{k[1]}"
            self.sems[k] = es.enter_context(self.nc.semaphore("sem_" + name))
        targets = {}
        for e in ENGINES:
            for waits, _, _, _, _, _ in self.thunks[e]:
                for f, c in waits:
                    targets.setdefault(f, set()).add(c)
        plans = {}
        for e in ENGINES:
            cum = 0
            pend = 0
            plan = []
            n = len(self.thunks[e])
            last_compute = -1
            for idx, (waits, fn, inc, tag, dma, key) in enumerate(self.thunks[e]):
                if dma:
                    plan.append([waits, fn, 16, tag, key])
                    continue
                cum += inc
                pend += inc
                attach = 0
                if idx == n - 1 or cum in targets.get(e, ()):
                    attach = pend
                    pend = 0
                plan.append([waits, fn, attach, tag, key])
                last_compute = len(plan) - 1
            if pend and last_compute >= 0:
                plan[last_compute][2] += pend
            plans[e] = plan

        block = es.enter_context(self.nc.Block())

        def mk(e):
            def run(eng):
                for waits, fn, attach, tag, key in plans[e]:
                    for f, c in waits:
                        eng.wait_ge(self.sems[f], c)
                    last = fn(eng)
                    if attach:
                        last.then_inc(self.sems[key], attach)
                if e == "sync":
                    for k in sorted(self.cnt.keys(), key=str):
                        if isinstance(k, tuple):
                            eng.wait_ge(self.sems[k], self.cnt[k])
            return run

        for e in ENGINES:
            getattr(block, e)(mk(e))


_off = 0
def _alloc(n):
    global _off
    r = _off
    _off += n
    return r

O_X0 = _alloc(2048)
O_X1 = _alloc(2048)
O_ACT = _alloc(8192)
O_QH = _alloc(8192)
O_VOW = _alloc(8192)
O_WB0 = _alloc(4096)
O_WB1 = _alloc(4096)
O_UB = _alloc(2048)
O_KVP = _alloc(2048)
O_KVB = _alloc(1024)
O_LN0 = _alloc(2048)
O_SCR = _alloc(2048)
O_QFM = _alloc(2048)
O_WAB0 = _alloc(1024)
O_WAB1 = _alloc(1024)
O_ROW = _alloc(2048)
O_DMP = O_SCR + 1024
ASZ = _off
assert ASZ == 52224, ASZ


_CC_HOOK = None


def build(debug=()):
    nc = bass.Bass("TRN2", target_bir_lowering=False, debug=False)

    x = nc.dram_tensor("x", [NT, D], F32, kind="ExternalInput").ap()
    wqk = nc.dram_tensor("wqk", [D, D], F32, kind="ExternalInput").ap()
    wv = nc.dram_tensor("wv", [D, D], F32, kind="ExternalInput").ap()
    wu = nc.dram_tensor("wu", [D, D], F32, kind="ExternalInput").ap()
    wo = nc.dram_tensor("wo", [D, D], F32, kind="ExternalInput").ap()
    vecs = nc.dram_tensor("vecs", [6, D], F32, kind="ExternalInput").ap()
    wfc = nc.dram_tensor("wfc", [H, D], F32, kind="ExternalInput").ap()
    wpj = nc.dram_tensor("wpj", [D, H], F32, kind="ExternalInput").ap()
    ident = nc.dram_tensor("ident", [128, 128], BF16, kind="ExternalInput").ap()
    ones = nc.dram_tensor("ones", [128, 128], F32, kind="ExternalInput").ap()
    y = nc.dram_tensor("y", [NT, D], F32, kind="ExternalOutput").ap()

    u_dram = nc.dram_tensor("u_dram", [TILES, 128, 4, 512], BF16, kind="Internal").ap()
    x1_dram = nc.dram_tensor("x1_dram", [TILES, 128, D], F32, kind="Internal").ap()
    cc_in = nc.dram_tensor("cc_in", [16 * 128, 128], F32, kind="Internal").ap()
    cc_out = nc.dram_tensor("cc_out", [16 * 128, 128], F32, kind="Internal").ap()

    dbg = {}
    for name in debug:
        shp = {"qh": [128, TILES * D], "vh": [128, TILES * D], "kv": [128, 16 * 128],
               "lnx": [128, TILES * D], "asb": [128, TILES * D], "x1": [NT, D],
               "r0": [128, GRP * NT], "fsb": [128, TILES * D], "kvb": [128, 16 * 128],
               "u": [128, TILES * D], "o": [128, TILES * D], "gb": [128, 2 * D],
               "sc": [128, 16], "cols": [128, 160], "ow": [128, TILES * D],
               "ub": [128, 4 * 4096], "ud": [4096, 512], "oln": [128, TILES * D],
               "mv1": [128, 8], "o2": [128, D]}[name]
        dt = BF16 if name in ("qh", "vh", "lnx", "r0", "kvb", "u", "ow", "ub", "ud") else F32
        dbg[name] = nc.dram_tensor("dbg_" + name, shp, dt, kind="ExternalOutput").ap()

    es = contextlib.ExitStack()
    arena = es.enter_context(nc.sbuf_tensor("arena", [128, ASZ], F32))
    af = arena.ap()
    ab = arena.ap().bitcast(BF16)

    id_sb = es.enter_context(nc.sbuf_tensor("id_sb", [128, 128], BF16))
    ones_sb = es.enter_context(nc.sbuf_tensor("ones_sb", [128, 128], F32))
    stats = es.enter_context(nc.sbuf_tensor("stats", [128, 2, 4, 6], F32))
    mv = es.enter_context(nc.sbuf_tensor("mv", [128, 2, 2], F32))
    mvsh = es.enter_context(nc.sbuf_tensor("mvsh", [128, 2, 2], F32))
    rstd = es.enter_context(nc.sbuf_tensor("rstd", [128, 2], F32))
    murstd = es.enter_context(nc.sbuf_tensor("murstd", [128, 2], F32))
    c_eps = es.enter_context(nc.sbuf_tensor("c_eps", [128, 1], F32))
    epsc = es.enter_context(nc.sbuf_tensor("epsc", [128, 1], F32))
    thr = es.enter_context(nc.sbuf_tensor("thr", [128, 4], F32))
    smalls = es.enter_context(nc.sbuf_tensor("smalls", [128, 8], F32))
    wcols = es.enter_context(nc.sbuf_tensor("wcols", [128, 80], F32))
    pcols = es.enter_context(nc.sbuf_tensor("pcols", [128, 80], F32))

    pA = es.enter_context(nc.psum_tensor("pA", [128, 2048], F32))
    pB = es.enter_context(nc.psum_tensor("pB", [128, 1536], F32))
    pT = es.enter_context(nc.psum_tensor("pT", [128, 1024], BF16))
    pT1 = pB.ap()[:, 1024:1536].bitcast(BF16)

    P = Prog(nc)

    def fview(off, n):
        return af[:, off:off + n]

    def bview(off, nbf):
        return ab[:, 2 * off:2 * off + nbf]

    def R(off, n):
        return ("ar", off, off + n)

    def PS(t, lo_col, n_col):
        base = {id(pA): 0, id(pB): 4}[id(t)]
        b0 = base + lo_col // 512
        b1 = base + (lo_col + n_col - 1) // 512 + 1
        return ("ps", b0, b1)

    PT0 = ("ps", 7, 8)
    PT1 = ("ps", 6, 7)

    x_rot = [fview(O_X0, 2048), fview(O_X1, 2048)]
    xr_res = [R(O_X0, 2048), R(O_X1, 2048)]
    act_fm = bview(O_ACT, 16384).rearrange("p (c t n) -> p c t n", c=16, t=8)

    def fm_iv(base, i, cs):
        return [("ar", base + c * 512 + i * 64, base + c * 512 + i * 64 + 64) for c in cs]
    qh = bview(O_QH, 16384).rearrange("p (t f) -> p t f", t=8)
    vh = bview(O_VOW, 16384).rearrange("p (t f) -> p t f", t=8)
    owfm = bview(O_VOW, 16384).rearrange("p (c t n) -> p c t n", c=16, t=8)
    wbuf = [bview(O_WB0, 8192).rearrange("p (c n) -> p c n", c=16),
            bview(O_WB1, 8192).rearrange("p (c n) -> p c n", c=16)]
    wbuf_res = [R(O_WB0, 4096), R(O_WB1, 4096)]
    ubuf = bview(O_UB, 4096).rearrange("p (t n) -> p t n", t=8)
    kv_part = fview(O_KVP, 2048).rearrange("p (h e) -> p h e", h=16)
    kv_bf = bview(O_KVB, 2048).rearrange("p (h e) -> p h e", h=16)
    ln0rot = [bview(O_LN0, 2048), bview(O_LN0 + 1024, 2048)]
    ln0_res = [R(O_LN0, 1024), R(O_LN0 + 1024, 1024)]
    scr = fview(O_SCR, 2048)
    scr_res = R(O_SCR, 2048)
    escr = fview(O_SCR, 512)
    e2 = fview(O_SCR + 512, 512)
    qfm = [bview(O_UB, 2048).rearrange("p (h e) -> p h e", h=16),
           bview(O_UB + 1024, 2048).rearrange("p (h e) -> p h e", h=16)]
    qfm_res = [R(O_UB, 1024), R(O_UB + 1024, 1024)]
    dump = fview(O_DMP, 1024)
    wabs = [fview(O_WAB0, 1024), fview(O_WAB1, 1024)]
    wabs_res = [R(O_WAB0, 1024), R(O_WAB1, 1024)]
    row = af[0:1, O_ROW:O_ROW + 2048]
    row_res = R(O_ROW, 2048)
    brep_full = fview(O_ROW, 2048)
    o_sb = fview(O_ROW, 2048)
    qtmp = [fview(O_QFM, 512), fview(O_QFM + 512, 512)]
    qtmp_res = [R(O_QFM, 512), R(O_QFM + 512, 512)]
    grep = fview(O_LN0, 2048)
    grep_res = R(O_LN0, 2048)
    brep = fview(O_KVP, 2048)
    brep_res = R(O_KVP, 2048)
    urel = [bview(O_QFM, 2048).rearrange("p (q n) -> p q n", q=4),
            bview(O_QFM + 1024, 2048).rearrange("p (q n) -> p q n", q=4)]
    urel_res = [R(O_QFM, 1024), R(O_QFM + 1024, 1024)]
    asb_off = [O_QH, O_QH + 2048, O_QH + 4096, O_QH + 6144, O_UB, O_KVP, O_LN0, O_X0]
    a_sb = [fview(o, 2048) for o in asb_off]
    a_res = [R(o, 2048) for o in asb_off]
    ln2st = [bview(O_QFM, 2048), bview(O_QFM + 1024, 2048)]
    fsb_off = [O_WB0, O_WB0 + 2048, O_WB1, O_WB1 + 2048, O_UB, O_KVP, O_LN0, O_X0]
    f_sb = [fview(o, 2048) for o in fsb_off]
    f_res = [R(o, 2048) for o in fsb_off]
    wpT = bview(O_QH, 12288).rearrange("p (s n) -> p s n", s=GRP)
    wpT_res = R(O_QH, 6144)
    wfT2_list = [bview(O_QH + 6144, 2048).rearrange("p (c n) -> p c n", c=16),
                 bview(O_WAB1, 2048).rearrange("p (c n) -> p c n", c=16)]
    wfT2_res = [R(O_QH + 6144, 1024), R(O_WAB1, 1024)]
    s1 = bview(O_QH + 7168, 2048)
    s1_res = R(O_QH + 7168, 1024)
    r_grp = bview(O_VOW, 6144).rearrange("p (s n) -> p s n", s=GRP)
    r_res = R(O_VOW, 3072)
    s2 = bview(O_VOW + 3072, 2048)
    s2_res = R(O_VOW + 3072, 1024)
    q2wp = bview(O_KVB, 2048).rearrange("p (c n) -> p c n", c=16)
    q2wp_flat = bview(O_KVB, 2048)
    q2wp_res = R(O_KVB, 1024)
    wfcb = [fview(O_VOW + 4096, 2048), fview(O_VOW + 6144, 2048)]
    wfcb_res = [R(O_VOW + 4096, 2048), R(O_VOW + 6144, 2048)]
    wpb = [fview(O_QFM, 2048).rearrange("p (c n) -> p c n", c=16),
           fview(O_WAB0, 2048).rearrange("p (c n) -> p c n", c=16)]
    wpb_flat = [fview(O_QFM, 2048), fview(O_WAB0, 2048)]
    wpb_res = [R(O_QFM, 2048), R(O_WAB0, 2048)]
    x1rel = fview(O_X1, 2048)

    ID = ("id", 0, 1)
    ONES = ("ones", 0, 1)
    SM = lambda name: (name, 0, 1)

    def ln_stats(slot, src_ap, src_res, eps_ap, eps_res, tag=""):
        def vf(v):
            for c in range(4):
                v.bn_stats(stats[:, slot, c, :], src_ap[:, c * 512:(c + 1) * 512])
            v.drain()
            v.bn_aggr(mv[:, slot, :], stats.ap()[:, slot].rearrange("p a b -> p (a b)"))
            v.drain()
            return v.tensor_copy(mvsh[:, slot, :], mv[:, slot, :])
        P.step("vector", vf, reads=[src_res], writes=[SM(f"mv{slot}")], tag="bn" + tag)

        def sf(s):
            s.activation(out=rstd[:, slot:slot + 1], in_=mv[:, slot, 1:2], func=AF.Ln, bias=eps_ap)
            return s.activation(out=rstd[:, slot:slot + 1], in_=rstd[:, slot:slot + 1], func=AF.Exp, scale=c_nh[:])
        P.step("scalar", sf, reads=[SM(f"mv{slot}"), eps_res, SM("c_nh")], writes=[SM(f"rs{slot}")], tag="rstd" + tag)

    def ln_apply(slot, src_ap, src_res, out_ap, out_res, tag=""):
        def vf(v):
            return v.tensor_scalar(out=out_ap, in0=src_ap, scalar1=mv[:, slot, 0:1],
                                   scalar2=rstd[:, slot:slot + 1], op0=ALU.subtract, op1=ALU.mult)
        P.step("vector", vf, reads=[src_res, SM(f"rs{slot}"), SM(f"mv{slot}")], writes=[out_res], tag="lnap" + tag)

    def transpose_batch(src_ap_fn, n, dst_ps, dst_ps_res, reads, tag=""):
        def tf(t):
            last = None
            for k in range(n):
                last = t.transpose(dst_ps[:, k * 128:(k + 1) * 128], src_ap_fn(k), id_sb[:])
            return last
        P.step("tensor", tf, reads=list(reads) + [ID], writes=[dst_ps_res], tag="tr" + tag)

    P.step("sync", lambda e: e.dma_start(out=id_sb[:], in_=ident), writes=[ID], dma=True)
    P.step("sync", lambda e: e.dma_start(out=ones_sb[:], in_=ones), writes=[ONES], dma=True)
    c_nh = es.enter_context(nc.sbuf_tensor("c_nh", [128, 1], F32))

    def init_consts(v):
        v.memset(c_eps[:], EPS)
        return v.memset(c_nh[:], -0.5)
    P.step("vector", init_consts, writes=[SM("c_eps"), SM("c_nh")])
    EPS_RES = SM("c_eps")

    abs_steps = []

    def make_abs_steps():
        idx = [0]
        wfc_r = wfc.rearrange("(a p) d -> p a d", p=128)
        wpj_r = wpj.rearrange("(a p) h -> p a h", p=128)
        chunks = []
        for a in range(40):
            for h2 in range(2):
                chunks.append(("w", wfc_r[:, a, h2 * 1024:(h2 + 1) * 1024], 2 * a + h2))
        for a in range(16):
            for k in range(5):
                chunks.append(("p", wpj_r[:, a, k * 1024:(k + 1) * 1024], 5 * a + k))
        for j, (kind, src, col) in enumerate(chunks):
            rot = j % 2
            cols = wcols if kind == "w" else pcols
            colres = SM("wcols") if kind == "w" else SM("pcols")

            def mk(rot=rot, src=src, cols=cols, col=col, colres=colres, j=j):
                def emit():
                    P.step("sync", lambda e: e.dma_start(out=wabs[rot], in_=src),
                           writes=[wabs_res[rot]], dma=True, tag=f"absdma{j}")
                    def sf(s):
                        return s.activation(out=dump[:, 0:1024], in_=wabs[rot], func=AF.Abs,
                                            accum_out=cols[:, col:col + 1])
                    P.step("scalar", sf, reads=[wabs_res[rot]],
                           writes=[R(O_DMP, 1024), colres], tag=f"absact{j}")
                return emit
            abs_steps.append(mk())
        return abs_steps

    make_abs_steps()
    abs_iter = iter(abs_steps)

    def drip_abs(n):
        for _ in range(n):
            s = next(abs_iter, None)
            if s:
                s()

    for i in range(TILES):
        rot = i % 2
        P.step("sync", lambda e, rot=rot, i=i: e.dma_start(out=x_rot[rot], in_=x[i * 128:(i + 1) * 128, :]),
               reads=[("x", i, i + 1)], writes=[xr_res[rot]], dma=True, tag=f"xdma{i}")
        ln_stats(rot, x_rot[rot], xr_res[rot], c_eps[:], EPS_RES, tag=f"ln0_{i}")
        ln_apply(rot, x_rot[rot], xr_res[rot], ln0rot[rot], ln0_res[rot], tag=f"ln0_{i}")
        for half in range(2):
            ps, psres = (pT, PT0) if half == 0 else (pT1, PT1)
            transpose_batch(lambda k, rot=rot, half=half: ln0rot[rot][:, (half * 8 + k) * 128:(half * 8 + k + 1) * 128],
                            8, ps, psres, [ln0_res[rot]], tag=f"ln0t{i}_{half}")
            P.step("vector", lambda v, ps=ps, i=i, half=half: v.tensor_copy(
                act_fm[:, half * 8:(half + 1) * 8, i, :],
                ps[:, 0:1024].rearrange("p (c n) -> p c n", c=8)),
                reads=[psres], writes=fm_iv(O_ACT, i, range(half * 8, half * 8 + 8)), tag=f"ln0c{i}_{half}")
        drip_abs(4)

    def repl_bias(name, bias_idx):
        P.step("sync", lambda e: e.dma_start(out=row, in_=vecs[bias_idx:bias_idx + 1, :]),
               reads=[("vecs", 0, 1)], writes=[row_res], dma=True, tag=f"brow_{name}")

        def bmm(t):
            last = None
            for q in range(4):
                last = t.matmul(pA[:, q * 512:(q + 1) * 512], ones_sb[0:1, :],
                                row[0:1, q * 512:(q + 1) * 512], start=True, stop=True)
            return last
        P.step("tensor", bmm, reads=[row_res, ONES], writes=[PS(pA, 0, 2048)], tag=f"bmm_{name}")
        P.step("vector", lambda v: v.tensor_copy(brep_full, pA[:]),
               reads=[PS(pA, 0, 2048)], writes=[row_res], tag=f"bcp_{name}")

    def wpass(name, wdram, bias_idx, evac, out_writes_fn):
        wr = wdram.rearrange("(c p) n -> p c n", p=128)
        repl_bias(name, bias_idx)
        for q in range(4):
            rot = q % 2
            wres_h = [(wbuf_res[rot][0], wbuf_res[rot][1], wbuf_res[rot][1] + 2048),
                      (wbuf_res[rot][0], wbuf_res[rot][1] + 2048, wbuf_res[rot][2])]
            for h in range(2):
                P.step("gpsimd", lambda e, rot=rot, q=q, h=h, wr=wr: e.dma_start(
                    out=wbuf[rot][:, h * 8:(h + 1) * 8, :], in_=wr[:, h * 8:(h + 1) * 8, q * 512:(q + 1) * 512]),
                    reads=[(name, q, q + 1)], writes=[wres_h[h]], dma=True, tag=f"wdma_{name}{q}_{h}")
            if q == 1 and _CC_HOOK is not None:
                _CC_HOOK()
            for i in range(TILES):
                b = (q * TILES + i) % 4
                psr = PS(pA, b * 512, 512)
                for h in range(2):
                    def mf(t, rot=rot, i=i, b=b, h=h):
                        last = None
                        for fc in range(h * 8, h * 8 + 8):
                            last = t.matmul(pA[:, b * 512:(b + 1) * 512], act_fm[:, fc, i, :],
                                            wbuf[rot][:, fc, :],
                                            start=(fc == 0), stop=(fc == DC - 1))
                        return last
                    P.step("tensor", mf, reads=[wres_h[h]] + fm_iv(O_ACT, i, range(h * 8, h * 8 + 8)),
                           writes=[psr], tag=f"mm_{name}{q}_{i}_{h}")
                evac(q, i, b, psr)
            drip_abs(10)

    def evac_qk(q, i, b, psr):
        rot2 = i % 2
        P.step("vector", lambda v, b=b, q=q, rot2=rot2: v.tensor_add(
            qtmp[rot2], pA[:, b * 512:(b + 1) * 512], brep_full[:, q * 512:(q + 1) * 512]),
            reads=[psr, row_res], writes=[qtmp_res[rot2]], tag=f"badd{q}_{i}")
        P.step("scalar", lambda s, rot2=rot2: s.activation(out=escr, in_=qtmp[rot2], func=AF.Exp),
               reads=[qtmp_res[rot2]], writes=[R(O_SCR, 512)], tag=f"eluexp{q}_{i}")

        def vf(v, q=q, i=i, rot2=rot2):
            v.tensor_scalar(out=e2, in0=escr, scalar1=1.0, scalar2=1.0, op0=ALU.min, op1=ALU.subtract)
            return v.scalar_tensor_tensor(out=qh[:, i, q * 512:(q + 1) * 512], in0=qtmp[rot2],
                                          scalar=0.0, in1=e2, op0=ALU.max, op1=ALU.add)
        P.step("vector", vf, reads=[R(O_SCR, 512), R(O_SCR + 512, 512), qtmp_res[rot2]],
               writes=[R(O_QH + i * 1024 + q * 256, 256), R(O_SCR + 512, 512)], tag=f"eluf{q}_{i}")

    wpass("wqk", wqk, 0, evac_qk, None)

    def evac_v(q, i, b, psr):
        rot2 = i % 2
        P.step("vector", lambda v, b=b, q=q, rot2=rot2: v.tensor_add(
            qtmp[rot2], pA[:, b * 512:(b + 1) * 512], brep_full[:, q * 512:(q + 1) * 512]),
            reads=[psr, row_res], writes=[qtmp_res[rot2]], tag=f"baddv{q}_{i}")
        P.step("scalar", lambda s, q=q, i=i, rot2=rot2: s.activation(
            out=vh[:, i, q * 512:(q + 1) * 512], in_=qtmp[rot2], func=AF.Silu),
            reads=[qtmp_res[rot2]], writes=[R(O_VOW + i * 1024 + q * 256, 256)], tag=f"silv{q}_{i}")

    wpass("wv", wv, 1, evac_v, None)

    slot_list = [(pA, 0), (pA, 512), (pA, 1024), (pA, 1536), (pB, 0), (pB, 512), (pB, 1024)]
    heads = list(range(16))
    hi = 0
    while hi < 16:
        rnd = heads[hi:hi + 7]
        for k, h in enumerate(rnd):
            t_, off = slot_list[k]
            psr = PS(t_, off, 128)

            def kf(t, t_=t_, off=off, h=h):
                last = None
                for i in range(TILES):
                    last = t.matmul(t_[:, off:off + 128], qh[:, i, h * 128:(h + 1) * 128],
                                    vh[:, i, h * 128:(h + 1) * 128], start=(i == 0), stop=(i == TILES - 1))
                return last
            P.step("tensor", kf, reads=[R(O_QH, 8192), R(O_VOW, 8192)], writes=[psr], tag=f"kv{h}")
        for k, h in enumerate(rnd):
            t_, off = slot_list[k]
            psr = PS(t_, off, 128)
            P.step("vector", lambda v, t_=t_, off=off, h=h: v.tensor_copy(kv_part[:, h, :], t_[:, off:off + 128]),
                   reads=[psr], writes=[R(O_KVP + h * 128, 128)], tag=f"kvc{h}")
        hi += 7
    P.step("sync", lambda e: e.dma_start(out=cc_in.rearrange("(h d) e -> d h e", d=128), in_=kv_part),
           reads=[R(O_KVP, 2048)], writes=[("cc_in", 0, 1)], dma=True, tag="cc_w")
    cc_sem = es.enter_context(nc.semaphore("cc_sem"))

    def emit_collective():
        def ccf(e):
            e.collective_compute(
                "AllReduce", ALU.add, replica_groups=[[0, 1, 2, 3], [4, 5, 6, 7]],
                ins=[cc_in.opt()], outs=[cc_out.opt()]).then_inc(cc_sem)
            e.wait_ge(cc_sem, 1)
            return e.nop()
        P.step("gpsimd", ccf, reads=[("cc_in", 0, 1)], writes=[("cc_out", 0, 1)], tag="ccar")
        P.step("gpsimd", lambda e: e.dma_start(out=kv_bf,
                                               in_=cc_out.rearrange("(h d) e -> d h e", d=128)),
               reads=[("cc_out", 0, 1)], writes=[R(O_KVB, 1024)], dma=True, tag="cc_r")
        if "kvb" in dbg:
            P.step("sync", lambda e: e.dma_start(out=dbg["kvb"], in_=kv_bf.rearrange("p h e -> p (h e)")),
                   reads=[R(O_KVB, 1024)], dma=True)

    def evac_u(q, i, b, psr):
        rot2 = i % 2
        P.step("vector", lambda v, b=b, q=q, rot2=rot2: v.tensor_add(
            qtmp[rot2], pA[:, b * 512:(b + 1) * 512], brep_full[:, q * 512:(q + 1) * 512]),
            reads=[psr, row_res], writes=[qtmp_res[rot2]], tag=f"baddu{q}_{i}")
        P.step("scalar", lambda s, i=i, rot2=rot2: s.activation(
            out=ubuf[:, i, :], in_=qtmp[rot2], func=AF.Silu),
            reads=[qtmp_res[rot2]], writes=[R(O_UB + i * 256, 256)], tag=f"silu{q}_{i}")
        if i == TILES - 1:
            P.step("sync", lambda e, q=q: e.dma_start(
                out=u_dram[:, :, q, :].rearrange("t p n -> p t n"), in_=ubuf[:]),
                reads=[R(O_UB, 2048)], writes=[("u_dram", q, q + 1)], dma=True, tag=f"udma{q}")
            if "ub" in dbg:
                P.step("sync", lambda e, q=q: e.dma_start(
                    out=dbg["ub"][:, q * 4096:(q + 1) * 4096],
                    in_=ubuf.rearrange("p t n -> p (t n)")),
                    reads=[R(O_UB, 2048)], dma=True)

    global _CC_HOOK
    _CC_HOOK = emit_collective
    wpass("wu", wu, 2, evac_u, None)
    _CC_HOOK = None

    drip_abs(200)

    if "qh" in dbg:
        P.step("sync", lambda e: e.dma_start(out=dbg["qh"], in_=qh.rearrange("p t f -> p (t f)")),
               reads=[R(O_QH, 8192)], dma=True)
    if "vh" in dbg:
        P.step("sync", lambda e: e.dma_start(out=dbg["vh"], in_=vh.rearrange("p t f -> p (t f)")),
               reads=[R(O_VOW, 8192)], dma=True)
    if "kv" in dbg:
        P.step("sync", lambda e: e.dma_start(out=dbg["kv"], in_=kv_part.rearrange("p h e -> p (h e)")),
               reads=[R(O_KVP, 2048)], dma=True)

    for j, (dst, dstres) in enumerate([(grep, grep_res), (brep, brep_res)]):
        P.step("sync", lambda e, j=j: e.dma_start(out=row, in_=vecs[4 + j:5 + j, :]),
               reads=[("vecs", 0, 1)], writes=[row_res], dma=True, tag=f"gbrow{j}")

        def gf(t, j=j):
            last = None
            for q in range(4):
                last = t.matmul(pA[:, q * 512:(q + 1) * 512], ones_sb[0:1, :],
                                row[0:1, q * 512:(q + 1) * 512], start=True, stop=True)
            return last
        P.step("tensor", gf, reads=[row_res, ONES], writes=[PS(pA, 0, 2048)], tag=f"gbmm{j}")
        P.step("vector", lambda v, dst=dst: v.tensor_copy(dst, pA[:]),
               reads=[PS(pA, 0, 2048)], writes=[dstres], tag=f"gbc{j}")

    if "ud" in dbg:
        P.step("gpsimd", lambda e: e.dma_start(out=dbg["ud"], in_=u_dram.rearrange("t p q n -> (t p q) n")),
               reads=[("u_dram", 0, 4)], dma=True)
    if "gb" in dbg:
        P.step("sync", lambda e: e.dma_start(out=dbg["gb"][:, 0:D], in_=grep), reads=[grep_res], dma=True)
        P.step("sync", lambda e: e.dma_start(out=dbg["gb"][:, D:2 * D], in_=brep), reads=[brep_res], dma=True)
    for i in range(TILES):
        rot = i % 2
        P.step("gpsimd", lambda e, rot=rot, i=i: e.dma_start(
            out=urel[rot], in_=u_dram[i, :, :, :]),
            reads=[("u_dram", 0, 4)], writes=[urel_res[rot]], dma=True, tag=f"urel{i}")
        for half in range(2):
            ps, psres = (pT, PT0) if half == 0 else (pT1, PT1)
            transpose_batch(lambda k, i=i, half=half: qh[:, i, (half * 8 + k) * 128:(half * 8 + k + 1) * 128],
                            8, ps, psres, [R(O_QH + i * 1024, 1024)], tag=f"qt{i}_{half}")
            P.step("vector", lambda v, ps=ps, rot=rot, half=half: v.tensor_copy(
                qfm[rot][:, half * 8:(half + 1) * 8, :],
                ps[:, 0:1024].rearrange("p (c n) -> p c n", c=8)),
                reads=[psres], writes=[qfm_res[rot]], tag=f"qtc{i}_{half}")
        if "u" in dbg:
            P.step("sync", lambda e, rot=rot, i=i: e.dma_start(
                out=dbg["u"][:, i * D:(i + 1) * D], in_=urel[rot].rearrange("p q n -> p (q n)")),
                reads=[urel_res[rot]], dma=True)
        def of(t, rot=rot):
            last = None
            for h in range(16):
                last = t.matmul(pA[:, h * 128:(h + 1) * 128], qfm[rot][:, h, :], kv_bf[:, h, :],
                                start=(h % 4 == 0), stop=(h % 4 == 3))
            return last
        P.step("tensor", of, reads=[qfm_res[rot], R(O_KVB, 1024)], writes=[PS(pA, 0, 2048)], tag=f"omm{i}")
        osb_i = x_rot[rot]
        osb_res = xr_res[rot]
        P.step("scalar", lambda s, osb_i=osb_i: s.copy(out=osb_i, in_=pA[:]),
               reads=[PS(pA, 0, 2048)], writes=[osb_res], tag=f"oevac{i}")
        if "o" in dbg:
            P.step("sync", lambda e, i=i, osb_i=osb_i: e.dma_start(out=dbg["o"][:, i * D:(i + 1) * D], in_=osb_i),
                   reads=[osb_res], dma=True)
        ln_stats(rot, osb_i, osb_res, c_eps[:], EPS_RES, tag=f"ln1_{i}")
        ln_apply(rot, osb_i, osb_res, osb_i, osb_res, tag=f"ln1_{i}")
        if "oln" in dbg:
            P.step("sync", lambda e, i=i, osb_i=osb_i: e.dma_start(out=dbg["oln"][:, i * D:(i + 1) * D], in_=osb_i),
                   reads=[osb_res], dma=True)
        owb_i = ab[:, 2 * (O_SCR + rot * 1024):2 * (O_SCR + rot * 1024) + 2048]
        owb_res = R(O_SCR + rot * 1024, 1024)

        P.step("vector", lambda v, osb_i=osb_i: v.tensor_mul(osb_i, osb_i, grep),
               reads=[osb_res, grep_res], writes=[osb_res], tag=f"gmul{i}")
        P.step("gpsimd", lambda g, osb_i=osb_i: g.tensor_add(osb_i, osb_i, brep),
               reads=[osb_res, brep_res], writes=[osb_res], tag=f"badd3_{i}")
        P.step("vector", lambda v, rot=rot, osb_i=osb_i, owb_i=owb_i: v.tensor_mul(
            owb_i, osb_i, urel[rot].rearrange("p q n -> p (q n)")),
            reads=[osb_res, urel_res[rot]], writes=[owb_res], tag=f"umul{i}")
        if "ow" in dbg:
            P.step("sync", lambda e, i=i, owb_i=owb_i: e.dma_start(out=dbg["ow"][:, i * D:(i + 1) * D],
                                                                   in_=owb_i),
                   reads=[owb_res], dma=True)
        for half in range(2):
            ps, psres = (pT, PT0) if half == 0 else (pT1, PT1)
            transpose_batch(lambda k, half=half, rot=rot: ab[:, 2 * (O_SCR + rot * 1024) + (half * 8 + k) * 128:
                                                             2 * (O_SCR + rot * 1024) + (half * 8 + k + 1) * 128],
                            8, ps, psres, [owb_res], tag=f"owt{i}_{half}")
            P.step("vector", lambda v, ps=ps, i=i, half=half: v.tensor_copy(
                owfm[:, half * 8:(half + 1) * 8, i, :],
                ps[:, 0:1024].rearrange("p (c n) -> p c n", c=8)),
                reads=[psres], writes=fm_iv(O_VOW, i, range(half * 8, half * 8 + 8)), tag=f"owc{i}_{half}")
        drip_abs(4)

    repl_bias("wo", 3)
    wor = wo.rearrange("(c p) n -> p c n", p=128)
    for q in range(4):
        rot = q % 2
        P.step("gpsimd", lambda e, rot=rot, q=q: e.dma_start(
            out=wbuf[rot][:], in_=wor[:, :, q * 512:(q + 1) * 512]),
            reads=[("wo", q, q + 1)], writes=[wbuf_res[rot]], dma=True, tag=f"wodma{q}")
        for i in range(TILES):
            b = (q * TILES + i) % 4
            psr = PS(pA, b * 512, 512)

            def mo(t, rot=rot, q=q, i=i, b=b):
                last = None
                for fc in range(DC):
                    last = t.matmul(pA[:, b * 512:(b + 1) * 512], owfm[:, fc, i, :], wbuf[rot][:, fc, :],
                                    start=(fc == 0), stop=(fc == DC - 1))
                return last
            P.step("tensor", mo, reads=[wbuf_res[rot]] + fm_iv(O_VOW, i, range(16)),
                   writes=[psr], tag=f"womm{q}_{i}")
            P.step("vector", lambda v, b=b, q=q, i=i: v.tensor_add(
                a_sb[i][:, q * 512:(q + 1) * 512], pA[:, b * 512:(b + 1) * 512],
                brep_full[:, q * 512:(q + 1) * 512]),
                reads=[psr, row_res], writes=[(a_res[i][0], a_res[i][1] + q * 512, a_res[i][1] + (q + 1) * 512)],
                tag=f"woev{q}_{i}")
        drip_abs(2)

    if "asb" in dbg:
        for i in range(TILES):
            P.step("sync", lambda e, i=i: e.dma_start(out=dbg["asb"][:, i * 2048:(i + 1) * 2048], in_=a_sb[i]),
                   reads=[a_res[i]], dma=True)

    for i in range(TILES):
        rot = i % 2
        P.step("sync", lambda e, i=i: e.dma_start(out=x1rel, in_=x[i * 128:(i + 1) * 128, :]),
               reads=[("x", i, i + 1)], writes=[R(O_X1, 2048)], dma=True, tag=f"xrel{i}")
        ln_stats(rot, a_sb[i], a_res[i], c_eps[:], EPS_RES, tag=f"lna_{i}")
        ln_apply(rot, a_sb[i], a_res[i], scr, scr_res, tag=f"lna_{i}")
        P.step("vector", lambda v: v.tensor_add(x1rel, scr, x1rel),
               reads=[scr_res, R(O_X1, 2048)], writes=[R(O_X1, 2048)], tag=f"x1add{i}")
        P.step("sync", lambda e, i=i: e.dma_start(out=x1_dram[i, :, :], in_=x1rel),
               reads=[R(O_X1, 2048)], writes=[("x1d", i, i + 1)], dma=True, tag=f"x1w{i}")
        ln_stats(rot, x1rel, R(O_X1, 2048), c_eps[:], EPS_RES, tag=f"ln2_{i}")
        ln_apply(rot, x1rel, R(O_X1, 2048), ln2st[rot], qfm_res[rot], tag=f"ln2_{i}")
        for half in range(2):
            ps, psres = (pT, PT0) if half == 0 else (pT1, PT1)
            transpose_batch(lambda k, rot=rot, half=half: ln2st[rot][:, (half * 8 + k) * 128:(half * 8 + k + 1) * 128],
                            8, ps, psres, [qfm_res[rot]], tag=f"ln2t{i}_{half}")
            P.step("vector", lambda v, ps=ps, i=i, half=half: v.tensor_copy(
                act_fm[:, half * 8:(half + 1) * 8, i, :],
                ps[:, 0:1024].rearrange("p (c n) -> p c n", c=8)),
                reads=[psres], writes=fm_iv(O_ACT, i, range(half * 8, half * 8 + 8)), tag=f"ln2c{i}_{half}")
        drip_abs(2)

    if "x1" in dbg:
        for i in range(TILES):
            P.step("gpsimd", lambda e, i=i: e.dma_start(out=dbg["x1"][i * 128:(i + 1) * 128, :], in_=x1_dram[i, :, :]),
                   reads=[("x1d", i, i + 1)], dma=True)

    drip_abs(200)

    P.barrier("barB2C")

    def redsum(v):
        v.tensor_reduce(out=smalls[:, 0:1], in_=wcols[:], axis=AX.X, op=ALU.add)
        v.drain()
        return v.tensor_reduce(out=smalls[:, 1:2], in_=pcols[:], axis=AX.X, op=ALU.add)
    P.step("vector", redsum, reads=[SM("wcols"), SM("pcols")], writes=[SM("sums")], tag="redsum")

    def totmm(t):
        t.matmul(pB[:, 1024:1025], ones_sb[:], smalls[:, 0:1], start=True, stop=True)
        return t.matmul(pB[:, 1028:1029], ones_sb[:], smalls[:, 1:2], start=True, stop=True)
    P.step("tensor", totmm, reads=[SM("sums"), ONES], writes=[PS(pB, 1024, 512)], tag="totmm")

    INV_N = float(np.float32(1.0 / NW))

    def chain(v):
        d = v.drain
        v.tensor_scalar(out=smalls[:, 2:3], in0=pB[:, 1024:1025], scalar1=INV_N, scalar2=EPS,
                        op0=ALU.mult, op1=ALU.max); d()
        v.tensor_scalar(out=smalls[:, 3:4], in0=pB[:, 1028:1029], scalar1=INV_N, scalar2=EPS,
                        op0=ALU.mult, op1=ALU.max); d()
        v.tensor_scalar(out=thr[:, 0:1], in0=smalls[:, 2:3], scalar1=0.7, scalar2=None, op0=ALU.mult); d()
        v.tensor_scalar(out=thr[:, 1:2], in0=thr[:, 0:1], scalar1=-1.0, scalar2=None, op0=ALU.mult); d()
        v.tensor_scalar(out=thr[:, 2:3], in0=smalls[:, 3:4], scalar1=0.7, scalar2=None, op0=ALU.mult); d()
        v.tensor_scalar(out=thr[:, 3:4], in0=thr[:, 2:3], scalar1=-1.0, scalar2=None, op0=ALU.mult); d()
        v.tensor_mul(smalls[:, 4:5], smalls[:, 2:3], smalls[:, 2:3]); d()
        v.tensor_mul(smalls[:, 5:6], smalls[:, 4:5], smalls[:, 3:4]); d()
        v.tensor_scalar(out=smalls[:, 6:7], in0=smalls[:, 5:6], scalar1=0.125, scalar2=None, op0=ALU.mult); d()
        v.reciprocal(smalls[:, 7:8], smalls[:, 6:7]); d()
        v.tensor_mul(smalls[:, 0:1], smalls[:, 7:8], smalls[:, 7:8]); d()
        v.tensor_scalar(out=epsc[:], in0=smalls[:, 0:1], scalar1=EPS, scalar2=None, op0=ALU.mult); d()
        return v.tensor_copy(smalls[:, 1:2], epsc[:])
    P.step("vector", chain, reads=[PS(pB, 1024, 512)], writes=[SM("thr"), SM("epsc"), SM("sums")], tag="chain")

    if "sc" in dbg:
        P.step("sync", lambda e: e.dma_start(out=dbg["sc"][:, 0:8], in_=smalls[:]),
               reads=[SM("thr"), SM("epsc"), SM("sums")], dma=True, tag="scdump1")
        P.step("sync", lambda e: e.dma_start(out=dbg["sc"][:, 8:12], in_=thr[:]),
               reads=[SM("thr")], dma=True, tag="scdump2")
    if "cols" in dbg:
        P.step("sync", lambda e: e.dma_start(out=dbg["cols"][:, 0:80], in_=wcols[:]),
               reads=[SM("wcols")], dma=True, tag="coldump1")
        P.step("sync", lambda e: e.dma_start(out=dbg["cols"][:, 80:160], in_=pcols[:]),
               reads=[SM("pcols")], dma=True, tag="coldump2")

    wfc_rows = wfc.rearrange("(a p) d -> p a d", p=128)
    wpj_cols = wpj.rearrange("(c p) h -> p c h", p=128)

    groups = [list(range(g, min(g + GRP, HCN))) for g in range(0, HCN, GRP)]
    for gi, grp in enumerate(groups):
        for sj, hc in enumerate(grp):
            rot = hc % 2
            P.step("sync", lambda e, rot=rot, hc=hc: e.dma_start(out=wfcb[rot], in_=wfc_rows[:, hc, :]),
                   reads=[("wfc", hc, hc + 1)], writes=[wfcb_res[rot]], dma=True, tag=f"fcd{hc}")
            P.step("sync", lambda e, rot=rot, hc=hc: e.dma_start(
                out=wpb[rot], in_=wpj_cols[:, :, hc * 128:(hc + 1) * 128]),
                reads=[("wpj", hc, hc + 1)], writes=[wpb_res[rot]], dma=True, tag=f"wpd{hc}")
            def tf1(s, rot=rot):
                s.activation(out=s1, in_=wfcb[rot], func=AF.Sign, bias=thr[:, 1:2])
                return s.activation(out=s2, in_=wfcb[rot], func=AF.Sign, bias=thr[:, 0:1])
            P.step("scalar", tf1, reads=[wfcb_res[rot], SM("thr")], writes=[s1_res, s2_res], tag=f"sgf{hc}")
            P.step("vector", lambda v: v.tensor_add(s1, s1, s2),
                   reads=[s1_res, s2_res], writes=[s1_res], tag=f"qf{hc}")
            def tf2(s, rot=rot):
                s.activation(out=q2wp_flat, in_=wpb_flat[rot], func=AF.Sign, bias=thr[:, 3:4])
                return s.activation(out=s2, in_=wpb_flat[rot], func=AF.Sign, bias=thr[:, 2:3])
            P.step("scalar", tf2, reads=[wpb_res[rot], SM("thr")], writes=[q2wp_res, s2_res], tag=f"sgp{hc}")
            P.step("vector", lambda v: v.tensor_add(q2wp_flat, q2wp_flat, s2),
                   reads=[q2wp_res, s2_res], writes=[q2wp_res], tag=f"qp{hc}")
            wfT = wfT2_list[hc % 2]
            wfT_res = wfT2_res[hc % 2]
            for half in range(2):
                ps, psres = (pT, PT0) if half == 0 else (pT1, PT1)
                transpose_batch(lambda k, half=half: s1[:, (half * 8 + k) * 128:(half * 8 + k + 1) * 128],
                                8, ps, psres, [s1_res], tag=f"ft{hc}_{half}")
                P.step("vector", lambda v, ps=ps, half=half, wfT=wfT: v.tensor_copy(
                    wfT[:, half * 8:(half + 1) * 8, :],
                    ps[:, 0:1024].rearrange("p (c n) -> p c n", c=8)),
                    reads=[psres], writes=[wfT_res], tag=f"ftc{hc}_{half}")
            for half in range(2):
                ps, psres = (pT, PT0) if half == 0 else (pT1, PT1)
                transpose_batch(lambda k, half=half: q2wp[:, half * 8 + k, :],
                                8, ps, psres, [q2wp_res], tag=f"pt{hc}_{half}")
                P.step("vector", lambda v, ps=ps, sj=sj, half=half: v.tensor_copy(
                    wpT[:, sj, half * 1024:(half + 1) * 1024], ps[:, 0:1024]),
                    reads=[psres], writes=[(wpT_res[0], wpT_res[1] + sj * 1024, wpT_res[1] + (sj + 1) * 1024)],
                    tag=f"ptc{hc}_{half}")
            for half in range(2):
                psr = PS(pB, half * 512, 512)

                def rf(t, half=half, wfT=wfT):
                    last = None
                    for dc in range(DC):
                        last = t.matmul(pB[:, half * 512:(half + 1) * 512], wfT[:, dc, :],
                                        act_fm[:, dc, half * 4:(half + 1) * 4, :].rearrange("p t n -> p (t n)"),
                                        start=(dc == 0), stop=(dc == DC - 1))
                    return last
                P.step("tensor", rf, reads=[wfT_res, R(O_ACT, 8192)], writes=[psr], tag=f"rmm{hc}_{half}")
                P.step("vector", lambda v, half=half: v.tensor_scalar(
                    out=scr[:, half * 512:(half + 1) * 512], in0=pB[:, half * 512:(half + 1) * 512],
                    scalar1=0.0, scalar2=None, op0=ALU.max),
                    reads=[psr], writes=[(("ar", O_SCR + half * 512, O_SCR + (half + 1) * 512))], tag=f"rr{hc}_{half}")
                P.step("vector", lambda v, sj=sj, half=half: v.tensor_mul(
                    r_grp[:, sj, half * 512:(half + 1) * 512], scr[:, half * 512:(half + 1) * 512],
                    scr[:, half * 512:(half + 1) * 512]),
                    reads=[("ar", O_SCR + half * 512, O_SCR + (half + 1) * 512), psr],
                    writes=[(r_res[0], r_res[1] + sj * 512, r_res[1] + (sj + 1) * 512)], tag=f"rsq{hc}_{half}")
        if "r0" in dbg and gi == 0:
            P.step("sync", lambda e: e.dma_start(out=dbg["r0"], in_=r_grp.rearrange("p s n -> p (s n)")),
                   reads=[r_res], dma=True)
        for i in range(TILES):
            for cg in range(4):
                psr = PS(pA, cg * 512, 512)

                def ff(t, i=i, cg=cg, grp=grp):
                    last = None
                    for sj in range(len(grp)):
                        last = t.matmul(pA[:, cg * 512:(cg + 1) * 512], r_grp[:, sj, i * 128:(i + 1) * 128],
                                        wpT[:, sj, cg * 512:(cg + 1) * 512],
                                        start=(sj == 0), stop=(sj == len(grp) - 1))
                    return last
                P.step("tensor", ff, reads=[r_res, wpT_res], writes=[psr], tag=f"fmm{gi}_{i}_{cg}")
            if gi == 0:
                P.step("vector", lambda v, i=i: v.tensor_copy(f_sb[i], pA[:]),
                       reads=[PS(pA, 0, 2048)], writes=[f_res[i]], tag=f"fc{gi}_{i}")
            else:
                P.step("vector", lambda v, i=i: v.tensor_add(f_sb[i], f_sb[i], pA[:]),
                       reads=[PS(pA, 0, 2048), f_res[i]], writes=[f_res[i]], tag=f"fa{gi}_{i}")

    if "fsb" in dbg:
        for i in range(TILES):
            P.step("sync", lambda e, i=i: e.dma_start(out=dbg["fsb"][:, i * 2048:(i + 1) * 2048], in_=f_sb[i]),
                   reads=[f_res[i]], dma=True)

    for i in range(TILES):
        rot = i % 2
        P.step("gpsimd", lambda e, i=i: e.dma_start(out=x1rel, in_=x1_dram[i, :, :]),
               reads=[("x1d", i, i + 1)], writes=[R(O_X1, 2048)], dma=True, tag=f"x1r{i}")
        ln_stats(rot, f_sb[i], f_res[i], epsc[:], SM("epsc"), tag=f"lnf_{i}")
        ln_apply(rot, f_sb[i], f_res[i], scr, scr_res, tag=f"lnf_{i}")
        P.step("vector", lambda v: v.tensor_add(scr, scr, x1rel),
               reads=[scr_res, R(O_X1, 2048)], writes=[scr_res], tag=f"oadd{i}")
        P.step("sync", lambda e, i=i: e.dma_start(out=y[i * 128:(i + 1) * 128, :], in_=scr),
               reads=[scr_res], writes=[("y", i, i + 1)], dma=True, tag=f"yw{i}")

    P.emit(es)
    es.close()
    return nc


_NC_CACHE = {}


def _get_nc(debug=()):
    key = tuple(sorted(debug))
    if key not in _NC_CACHE:
        _NC_CACHE[key] = build(debug=key)
    return _NC_CACHE[key]


def kernel(L0_attn_x, L0_attn_Wqk, L0_attn_bqk, L0_attn_Wv, L0_attn_bv,
           L0_attn_Wu, L0_attn_bu, L0_attn_Wo, L0_attn_bo,
           L0_attn_ln_g, L0_attn_ln_b, L0_ffn_w_fc, L0_ffn_w_proj,
           _debug=(), _results_hook=None):
    f32 = np.float32
    xfull = np.ascontiguousarray(np.asarray(L0_attn_x, f32).reshape(8 * NT, D))
    vecs = np.ascontiguousarray(np.stack([
        np.asarray(L0_attn_bqk, f32), np.asarray(L0_attn_bv, f32),
        np.asarray(L0_attn_bu, f32), np.asarray(L0_attn_bo, f32),
        np.asarray(L0_attn_ln_g, f32), np.asarray(L0_attn_ln_b, f32)], axis=0))
    base = dict(
        wqk=np.ascontiguousarray(np.asarray(L0_attn_Wqk, f32)),
        wv=np.ascontiguousarray(np.asarray(L0_attn_Wv, f32)),
        wu=np.ascontiguousarray(np.asarray(L0_attn_Wu, f32)),
        wo=np.ascontiguousarray(np.asarray(L0_attn_Wo, f32)),
        vecs=vecs,
        wfc=np.ascontiguousarray(np.asarray(L0_ffn_w_fc, f32)),
        wpj=np.ascontiguousarray(np.asarray(L0_ffn_w_proj, f32)),
        ident=np.eye(128, dtype=BF),
        ones=np.ones((128, 128), dtype=f32),
    )
    nc = _get_nc(_debug)
    in_maps = [dict(base, x=np.ascontiguousarray(xfull[i * NT:(i + 1) * NT])) for i in range(8)]
    res = run_bass_kernel_spmd(nc, in_maps, core_ids=list(range(8)))
    if _results_hook is not None:
        _results_hook(res)
    out = np.concatenate([np.asarray(res.results[i]["y"]) for i in range(8)], axis=0)
    return out.reshape(2, 4096, D).astype(np.float32)


# version 16 (final)
# speedup vs baseline: 1.0255x; 1.0234x over previous
import contextlib
import numpy as np
import ml_dtypes
import concourse.bass as bass
import concourse.mybir as mybir
from concourse.bass_utils import run_bass_kernel_spmd

F32 = mybir.dt.float32
BF16 = mybir.dt.bfloat16
AF = mybir.ActivationFunctionType
ALU = mybir.AluOpType
AX = mybir.AxisListType
BF = ml_dtypes.bfloat16

D = 2048
H = 5120
NT = 1024
TILES = 8
DC = 16
HCN = 40
EPS = 1e-5
NW = 5120 * 2048
GRP = 8

ENGINES = ("tensor", "vector", "scalar", "gpsimd", "sync")


NLANES = 8


class Prog:

    def __init__(self, nc):
        self.nc = nc
        self.thunks = {e: [] for e in ENGINES}
        self.cnt = {}
        self.lane_idx = {e: 0 for e in ENGINES}
        self.lastwait = {e: {} for e in ENGINES}
        self.state = []
        self.sems = {}

    def step(self, eng, fn, reads=(), writes=(), dma=False, tag=""):
        if dma:
            key = (eng, self.lane_idx[eng] % NLANES)
            self.lane_idx[eng] += 1
        else:
            key = eng
        need = {}

        def bump(f, c):
            if f != key and c > need.get(f, 0):
                need[f] = c

        for (k, lo, hi) in reads:
            for st in self.state:
                if st[0] == k and st[2] > lo and st[1] < hi and st[3] is not None:
                    bump(*st[3])
        for (k, lo, hi) in writes:
            for st in self.state:
                if st[0] == k and st[2] > lo and st[1] < hi:
                    if st[3] is not None:
                        bump(*st[3])
                    for f, c in st[4].items():
                        bump(f, c)
        waits = []
        lw = self.lastwait[eng]
        for f, c in sorted(need.items(), key=str):
            if isinstance(f, str) and f == eng:
                continue
            if lw.get(f, 0) < c:
                waits.append((f, c))
                lw[f] = c
        inc = 16 if dma else 1
        self.cnt[key] = self.cnt.get(key, 0) + inc
        cend = self.cnt[key]
        for (k, lo, hi) in writes:
            keep = []
            for st in self.state:
                if st[0] == k and st[1] >= lo and st[2] <= hi:
                    continue
                keep.append(st)
            self.state = keep
            self.state.append([k, lo, hi, (key, cend), {}])
        for (k, lo, hi) in reads:
            hit = False
            for st in self.state:
                if st[0] == k and st[2] > lo and st[1] < hi:
                    st[4][key] = cend
                    hit = True
            if not hit:
                self.state.append([k, lo, hi, None, {key: cend}])
        self.thunks[eng].append([waits, fn, inc, tag, dma, key])

    def barrier(self, tag="bar"):
        snap_comp = {e: self.cnt.get(e, 0) for e in ENGINES}
        snap_lane = {k: v for k, v in self.cnt.items() if isinstance(k, tuple)}

        def mk(eng):
            def fn(e):
                for f, c in sorted(snap_comp.items()):
                    if f != eng and c:
                        e.wait_ge(self.sems[f], c)
                for f, c in sorted(snap_lane.items(), key=str):
                    e.wait_ge(self.sems[f], c)
                return e.nop()
            return fn
        for eng in ENGINES:
            self.step(eng, mk(eng), tag=f"{tag}_{eng}")
        for st in self.state:
            pass

    def emit(self, es):
        keys = set(self.cnt.keys())
        for e in ENGINES:
            keys.add(e)
        for k in sorted(keys, key=str):
            name = k if isinstance(k, str) else f"{k[0]}_l{k[1]}"
            self.sems[k] = es.enter_context(self.nc.semaphore("sem_" + name))
        targets = {}
        for e in ENGINES:
            for waits, _, _, _, _, _ in self.thunks[e]:
                for f, c in waits:
                    targets.setdefault(f, set()).add(c)
        plans = {}
        for e in ENGINES:
            cum = 0
            pend = 0
            plan = []
            n = len(self.thunks[e])
            last_compute = -1
            for idx, (waits, fn, inc, tag, dma, key) in enumerate(self.thunks[e]):
                if dma:
                    plan.append([waits, fn, 16, tag, key])
                    continue
                cum += inc
                pend += inc
                attach = 0
                if idx == n - 1 or cum in targets.get(e, ()):
                    attach = pend
                    pend = 0
                plan.append([waits, fn, attach, tag, key])
                last_compute = len(plan) - 1
            if pend and last_compute >= 0:
                plan[last_compute][2] += pend
            plans[e] = plan

        block = es.enter_context(self.nc.Block())

        def mk(e):
            def run(eng):
                for waits, fn, attach, tag, key in plans[e]:
                    for f, c in waits:
                        eng.wait_ge(self.sems[f], c)
                    last = fn(eng)
                    if attach:
                        last.then_inc(self.sems[key], attach)
                if e == "sync":
                    for k in sorted(self.cnt.keys(), key=str):
                        if isinstance(k, tuple):
                            eng.wait_ge(self.sems[k], self.cnt[k])
            return run

        for e in ENGINES:
            getattr(block, e)(mk(e))


_off = 0
def _alloc(n):
    global _off
    r = _off
    _off += n
    return r

O_X0 = _alloc(2048)
O_X1 = _alloc(2048)
O_ACT = _alloc(8192)
O_QH = _alloc(8192)
O_VOW = _alloc(8192)
O_WB0 = _alloc(4096)
O_WB1 = _alloc(4096)
O_UB = _alloc(2048)
O_KVP = _alloc(2048)
O_KVB = _alloc(1024)
O_LN0 = _alloc(2048)
O_SCR = _alloc(2048)
O_QFM = _alloc(2048)
O_WAB0 = _alloc(1024)
O_WAB1 = _alloc(1024)
O_ROW = _alloc(2048)
O_DMP = O_SCR + 1024
ASZ = _off
assert ASZ == 52224, ASZ


_CC_HOOK = None


def build(debug=()):
    nc = bass.Bass("TRN2", target_bir_lowering=False, debug=False)

    x = nc.dram_tensor("x", [NT, D], F32, kind="ExternalInput").ap()
    wqk = nc.dram_tensor("wqk", [D, D], F32, kind="ExternalInput").ap()
    wv = nc.dram_tensor("wv", [D, D], F32, kind="ExternalInput").ap()
    wu = nc.dram_tensor("wu", [D, D], F32, kind="ExternalInput").ap()
    wo = nc.dram_tensor("wo", [D, D], F32, kind="ExternalInput").ap()
    vecs = nc.dram_tensor("vecs", [6, D], F32, kind="ExternalInput").ap()
    wfc = nc.dram_tensor("wfc", [H, D], F32, kind="ExternalInput").ap()
    wpj = nc.dram_tensor("wpj", [D, H], F32, kind="ExternalInput").ap()
    ident = nc.dram_tensor("ident", [128, 128], BF16, kind="ExternalInput").ap()
    ones = nc.dram_tensor("ones", [128, 128], F32, kind="ExternalInput").ap()
    y = nc.dram_tensor("y", [NT, D], F32, kind="ExternalOutput").ap()

    u_dram = nc.dram_tensor("u_dram", [TILES, 128, 4, 512], BF16, kind="Internal").ap()
    x1_dram = nc.dram_tensor("x1_dram", [TILES, 128, D], F32, kind="Internal").ap()
    cc_in = nc.dram_tensor("cc_in", [16 * 128, 128], F32, kind="Internal").ap()
    cc_out = nc.dram_tensor("cc_out", [16 * 128, 128], F32, kind="Internal").ap()

    dbg = {}
    for name in debug:
        shp = {"qh": [128, TILES * D], "vh": [128, TILES * D], "kv": [128, 16 * 128],
               "lnx": [128, TILES * D], "asb": [128, TILES * D], "x1": [NT, D],
               "r0": [128, GRP * NT], "fsb": [128, TILES * D], "kvb": [128, 16 * 128],
               "u": [128, TILES * D], "o": [128, TILES * D], "gb": [128, 2 * D],
               "sc": [128, 16], "cols": [128, 160], "ow": [128, TILES * D],
               "ub": [128, 4 * 4096], "ud": [4096, 512], "oln": [128, TILES * D],
               "mv1": [128, 8], "o2": [128, D]}[name]
        dt = BF16 if name in ("qh", "vh", "lnx", "r0", "kvb", "u", "ow", "ub", "ud") else F32
        dbg[name] = nc.dram_tensor("dbg_" + name, shp, dt, kind="ExternalOutput").ap()

    es = contextlib.ExitStack()
    arena = es.enter_context(nc.sbuf_tensor("arena", [128, ASZ], F32))
    af = arena.ap()
    ab = arena.ap().bitcast(BF16)

    id_sb = es.enter_context(nc.sbuf_tensor("id_sb", [128, 128], BF16))
    ones_sb = es.enter_context(nc.sbuf_tensor("ones_sb", [128, 128], F32))
    stats = es.enter_context(nc.sbuf_tensor("stats", [128, 2, 4, 6], F32))
    mv = es.enter_context(nc.sbuf_tensor("mv", [128, 2, 2], F32))
    mvsh = es.enter_context(nc.sbuf_tensor("mvsh", [128, 2, 2], F32))
    rstd = es.enter_context(nc.sbuf_tensor("rstd", [128, 2], F32))
    murstd = es.enter_context(nc.sbuf_tensor("murstd", [128, 2], F32))
    c_eps = es.enter_context(nc.sbuf_tensor("c_eps", [128, 1], F32))
    epsc = es.enter_context(nc.sbuf_tensor("epsc", [128, 1], F32))
    thr = es.enter_context(nc.sbuf_tensor("thr", [128, 4], F32))
    smalls = es.enter_context(nc.sbuf_tensor("smalls", [128, 8], F32))
    wcols = es.enter_context(nc.sbuf_tensor("wcols", [128, 80], F32))
    pcols = es.enter_context(nc.sbuf_tensor("pcols", [128, 80], F32))

    pA = es.enter_context(nc.psum_tensor("pA", [128, 2048], F32))
    pB = es.enter_context(nc.psum_tensor("pB", [128, 1536], F32))
    pT = es.enter_context(nc.psum_tensor("pT", [128, 1024], BF16))
    pT1 = pB.ap()[:, 1024:1536].bitcast(BF16)

    P = Prog(nc)

    def fview(off, n):
        return af[:, off:off + n]

    def bview(off, nbf):
        return ab[:, 2 * off:2 * off + nbf]

    def R(off, n):
        return ("ar", off, off + n)

    def PS(t, lo_col, n_col):
        base = {id(pA): 0, id(pB): 4}[id(t)]
        b0 = base + lo_col // 512
        b1 = base + (lo_col + n_col - 1) // 512 + 1
        return ("ps", b0, b1)

    PT0 = ("ps", 7, 8)
    PT1 = ("ps", 6, 7)

    x_rot = [fview(O_X0, 2048), fview(O_X1, 2048)]
    xr_res = [R(O_X0, 2048), R(O_X1, 2048)]
    act_fm = bview(O_ACT, 16384).rearrange("p (c t n) -> p c t n", c=16, t=8)
    qh = bview(O_QH, 16384).rearrange("p (t f) -> p t f", t=8)
    vh = bview(O_VOW, 16384).rearrange("p (t f) -> p t f", t=8)
    owfm = bview(O_VOW, 16384).rearrange("p (c t n) -> p c t n", c=16, t=8)
    wbuf = [bview(O_WB0, 8192).rearrange("p (c n) -> p c n", c=16),
            bview(O_WB1, 8192).rearrange("p (c n) -> p c n", c=16)]
    wbuf_res = [R(O_WB0, 4096), R(O_WB1, 4096)]
    ubuf = bview(O_UB, 4096).rearrange("p (t n) -> p t n", t=8)
    kv_part = fview(O_KVP, 2048).rearrange("p (h e) -> p h e", h=16)
    kv_bf = bview(O_KVB, 2048).rearrange("p (h e) -> p h e", h=16)
    ln0rot = [bview(O_LN0, 2048), bview(O_LN0 + 1024, 2048)]
    ln0_res = [R(O_LN0, 1024), R(O_LN0 + 1024, 1024)]
    scr = fview(O_SCR, 2048)
    scr_res = R(O_SCR, 2048)
    escr = fview(O_SCR, 512)
    e2 = fview(O_SCR + 512, 512)
    qfm = [bview(O_UB, 2048).rearrange("p (h e) -> p h e", h=16),
           bview(O_UB + 1024, 2048).rearrange("p (h e) -> p h e", h=16)]
    qfm_res = [R(O_UB, 1024), R(O_UB + 1024, 1024)]
    dump = fview(O_DMP, 1024)
    wabs = [fview(O_WAB0, 1024), fview(O_WAB1, 1024)]
    wabs_res = [R(O_WAB0, 1024), R(O_WAB1, 1024)]
    row = af[0:1, O_ROW:O_ROW + 2048]
    row_res = R(O_ROW, 2048)
    brep_full = fview(O_ROW, 2048)
    o_sb = fview(O_ROW, 2048)
    qtmp = [fview(O_QFM, 512), fview(O_QFM + 512, 512)]
    qtmp_res = [R(O_QFM, 512), R(O_QFM + 512, 512)]
    grep = fview(O_LN0, 2048)
    grep_res = R(O_LN0, 2048)
    brep = fview(O_KVP, 2048)
    brep_res = R(O_KVP, 2048)
    urel = [bview(O_QFM, 2048).rearrange("p (q n) -> p q n", q=4),
            bview(O_QFM + 1024, 2048).rearrange("p (q n) -> p q n", q=4)]
    urel_res = [R(O_QFM, 1024), R(O_QFM + 1024, 1024)]
    asb_off = [O_QH, O_QH + 2048, O_QH + 4096, O_QH + 6144, O_UB, O_KVP, O_LN0, O_X0]
    a_sb = [fview(o, 2048) for o in asb_off]
    a_res = [R(o, 2048) for o in asb_off]
    ln2st = [bview(O_QFM, 2048), bview(O_QFM + 1024, 2048)]
    fsb_off = [O_WB0, O_WB0 + 2048, O_WB1, O_WB1 + 2048, O_UB, O_KVP, O_LN0, O_X0]
    f_sb = [fview(o, 2048) for o in fsb_off]
    f_res = [R(o, 2048) for o in fsb_off]
    _wpT6 = bview(O_QH, 12288).rearrange("p (s n) -> p s n", s=6)
    _wpT_x = [bview(O_ROW, 2048), bview(O_ROW + 1024, 2048)]

    def wpT_s(sj):
        return _wpT6[:, sj, :] if sj < 6 else _wpT_x[sj - 6]

    def wpT_r(sj):
        return R(O_QH + sj * 1024, 1024) if sj < 6 else R(O_ROW + (sj - 6) * 1024, 1024)
    wpT_all = [wpT_r(s) for s in range(GRP)]
    wfT2_list = [bview(O_QH + 6144, 2048).rearrange("p (c n) -> p c n", c=16),
                 bview(O_WAB1, 2048).rearrange("p (c n) -> p c n", c=16)]
    wfT2_res = [R(O_QH + 6144, 1024), R(O_WAB1, 1024)]
    s1 = bview(O_QH + 7168, 2048)
    s1_res = R(O_QH + 7168, 1024)
    _r6 = bview(O_VOW, 6144).rearrange("p (s n) -> p s n", s=6)
    _r_x = [bview(O_X1, 1024), bview(O_X1 + 512, 1024)]

    def r_s(sj):
        return _r6[:, sj, :] if sj < 6 else _r_x[sj - 6]

    def r_r(sj):
        return R(O_VOW + sj * 512, 512) if sj < 6 else R(O_X1 + (sj - 6) * 512, 512)
    r_all = [r_r(s) for s in range(GRP)]
    s2 = bview(O_VOW + 3072, 2048)
    s2_res = R(O_VOW + 3072, 1024)
    q2wp = bview(O_KVB, 2048).rearrange("p (c n) -> p c n", c=16)
    q2wp_flat = bview(O_KVB, 2048)
    q2wp_res = R(O_KVB, 1024)
    wfcb = [fview(O_VOW + 4096, 2048), fview(O_VOW + 6144, 2048)]
    wfcb_res = [R(O_VOW + 4096, 2048), R(O_VOW + 6144, 2048)]
    wpb = [fview(O_QFM, 2048).rearrange("p (c n) -> p c n", c=16),
           fview(O_WAB0, 2048).rearrange("p (c n) -> p c n", c=16)]
    wpb_flat = [fview(O_QFM, 2048), fview(O_WAB0, 2048)]
    wpb_res = [R(O_QFM, 2048), R(O_WAB0, 2048)]
    x1rel = fview(O_X1, 2048)

    ID = ("id", 0, 1)
    ONES = ("ones", 0, 1)
    SM = lambda name: (name, 0, 1)

    def ln_stats(slot, src_ap, src_res, eps_ap, eps_res, tag=""):
        def vf(v):
            for c in range(4):
                v.bn_stats(stats[:, slot, c, :], src_ap[:, c * 512:(c + 1) * 512])
            v.drain()
            v.bn_aggr(mv[:, slot, :], stats.ap()[:, slot].rearrange("p a b -> p (a b)"))
            v.drain()
            return v.tensor_copy(mvsh[:, slot, :], mv[:, slot, :])
        P.step("vector", vf, reads=[src_res], writes=[SM(f"mv{slot}")], tag="bn" + tag)

        def sf(s):
            s.activation(out=rstd[:, slot:slot + 1], in_=mv[:, slot, 1:2], func=AF.Ln, bias=eps_ap)
            return s.activation(out=rstd[:, slot:slot + 1], in_=rstd[:, slot:slot + 1], func=AF.Exp, scale=c_nh[:])
        P.step("scalar", sf, reads=[SM(f"mv{slot}"), eps_res, SM("c_nh")], writes=[SM(f"rs{slot}")], tag="rstd" + tag)

    def ln_apply(slot, src_ap, src_res, out_ap, out_res, tag=""):
        def vf(v):
            return v.tensor_scalar(out=out_ap, in0=src_ap, scalar1=mv[:, slot, 0:1],
                                   scalar2=rstd[:, slot:slot + 1], op0=ALU.subtract, op1=ALU.mult)
        P.step("vector", vf, reads=[src_res, SM(f"rs{slot}"), SM(f"mv{slot}")], writes=[out_res], tag="lnap" + tag)

    def transpose_batch(src_ap_fn, n, dst_ps, dst_ps_res, reads, tag=""):
        def tf(t):
            last = None
            for k in range(n):
                last = t.transpose(dst_ps[:, k * 128:(k + 1) * 128], src_ap_fn(k), id_sb[:])
            return last
        P.step("tensor", tf, reads=list(reads) + [ID], writes=[dst_ps_res], tag="tr" + tag)

    P.step("sync", lambda e: e.dma_start(out=id_sb[:], in_=ident), writes=[ID], dma=True)
    P.step("sync", lambda e: e.dma_start(out=ones_sb[:], in_=ones), writes=[ONES], dma=True)
    c_nh = es.enter_context(nc.sbuf_tensor("c_nh", [128, 1], F32))

    def init_consts(v):
        v.memset(c_eps[:], EPS)
        return v.memset(c_nh[:], -0.5)
    P.step("vector", init_consts, writes=[SM("c_eps"), SM("c_nh")])
    EPS_RES = SM("c_eps")

    abs_steps = []

    def make_abs_steps():
        idx = [0]
        wfc_r = wfc.rearrange("(a p) d -> p a d", p=128)
        wpj_r = wpj.rearrange("(a p) h -> p a h", p=128)
        chunks = []
        for a in range(40):
            for h2 in range(2):
                chunks.append(("w", wfc_r[:, a, h2 * 1024:(h2 + 1) * 1024], 2 * a + h2))
        for a in range(16):
            for k in range(5):
                chunks.append(("p", wpj_r[:, a, k * 1024:(k + 1) * 1024], 5 * a + k))
        for j, (kind, src, col) in enumerate(chunks):
            rot = j % 2
            cols = wcols if kind == "w" else pcols
            colres = SM("wcols") if kind == "w" else SM("pcols")

            def mk(rot=rot, src=src, cols=cols, col=col, colres=colres, j=j):
                def emit():
                    P.step("sync", lambda e: e.dma_start(out=wabs[rot], in_=src),
                           writes=[wabs_res[rot]], dma=True, tag=f"absdma{j}")
                    def sf(s):
                        return s.activation(out=dump[:, 0:1024], in_=wabs[rot], func=AF.Abs,
                                            accum_out=cols[:, col:col + 1])
                    P.step("scalar", sf, reads=[wabs_res[rot]],
                           writes=[R(O_DMP, 1024), colres], tag=f"absact{j}")
                return emit
            abs_steps.append(mk())
        return abs_steps

    make_abs_steps()
    abs_iter = iter(abs_steps)

    def drip_abs(n):
        for _ in range(n):
            s = next(abs_iter, None)
            if s:
                s()

    for i in range(TILES):
        rot = i % 2
        P.step("sync", lambda e, rot=rot, i=i: e.dma_start(out=x_rot[rot], in_=x[i * 128:(i + 1) * 128, :]),
               reads=[("x", i, i + 1)], writes=[xr_res[rot]], dma=True, tag=f"xdma{i}")
        ln_stats(rot, x_rot[rot], xr_res[rot], c_eps[:], EPS_RES, tag=f"ln0_{i}")
        ln_apply(rot, x_rot[rot], xr_res[rot], ln0rot[rot], ln0_res[rot], tag=f"ln0_{i}")
        for half in range(2):
            ps, psres = (pT, PT0) if half == 0 else (pT1, PT1)
            transpose_batch(lambda k, rot=rot, half=half: ln0rot[rot][:, (half * 8 + k) * 128:(half * 8 + k + 1) * 128],
                            8, ps, psres, [ln0_res[rot]], tag=f"ln0t{i}_{half}")
            P.step("vector", lambda v, ps=ps, i=i, half=half: v.tensor_copy(
                act_fm[:, half * 8:(half + 1) * 8, i, :],
                ps[:, 0:1024].rearrange("p (c n) -> p c n", c=8)),
                reads=[psres], writes=[R(O_ACT + (half * 8) * 512, 4096)], tag=f"ln0c{i}_{half}")
        drip_abs(4)

    def repl_bias(name, bias_idx):
        P.step("sync", lambda e: e.dma_start(out=row, in_=vecs[bias_idx:bias_idx + 1, :]),
               reads=[("vecs", 0, 1)], writes=[row_res], dma=True, tag=f"brow_{name}")

        def bmm(t):
            last = None
            for q in range(4):
                last = t.matmul(pA[:, q * 512:(q + 1) * 512], ones_sb[0:1, :],
                                row[0:1, q * 512:(q + 1) * 512], start=True, stop=True)
            return last
        P.step("tensor", bmm, reads=[row_res, ONES], writes=[PS(pA, 0, 2048)], tag=f"bmm_{name}")
        P.step("vector", lambda v: v.tensor_copy(brep_full, pA[:]),
               reads=[PS(pA, 0, 2048)], writes=[row_res], tag=f"bcp_{name}")

    def wpass(name, wdram, bias_idx, evac, out_writes_fn):
        wr = wdram.rearrange("(c p) n -> p c n", p=128)
        repl_bias(name, bias_idx)
        for q in range(4):
            rot = q % 2
            wres_h = [(wbuf_res[rot][0], wbuf_res[rot][1], wbuf_res[rot][1] + 2048),
                      (wbuf_res[rot][0], wbuf_res[rot][1] + 2048, wbuf_res[rot][2])]
            for h in range(2):
                P.step("gpsimd", lambda e, rot=rot, q=q, h=h, wr=wr: e.dma_start(
                    out=wbuf[rot][:, h * 8:(h + 1) * 8, :], in_=wr[:, h * 8:(h + 1) * 8, q * 512:(q + 1) * 512]),
                    reads=[(name, q, q + 1)], writes=[wres_h[h]], dma=True, tag=f"wdma_{name}{q}_{h}")
            if q == 1 and _CC_HOOK is not None:
                _CC_HOOK()
            for i in range(TILES):
                b = (q * TILES + i) % 4
                psr = PS(pA, b * 512, 512)
                for h in range(2):
                    def mf(t, rot=rot, i=i, b=b, h=h):
                        last = None
                        for fc in range(h * 8, h * 8 + 8):
                            last = t.matmul(pA[:, b * 512:(b + 1) * 512], act_fm[:, fc, i, :],
                                            wbuf[rot][:, fc, :],
                                            start=(fc == 0), stop=(fc == DC - 1))
                        return last
                    P.step("tensor", mf, reads=[wres_h[h], R(O_ACT, 8192)],
                           writes=[psr], tag=f"mm_{name}{q}_{i}_{h}")
                evac(q, i, b, psr)
            drip_abs(10)

    def evac_qk(q, i, b, psr):
        rot2 = i % 2
        P.step("vector", lambda v, b=b, q=q, rot2=rot2: v.tensor_add(
            qtmp[rot2], pA[:, b * 512:(b + 1) * 512], brep_full[:, q * 512:(q + 1) * 512]),
            reads=[psr, row_res], writes=[qtmp_res[rot2]], tag=f"badd{q}_{i}")
        P.step("scalar", lambda s, rot2=rot2: s.activation(out=escr, in_=qtmp[rot2], func=AF.Exp),
               reads=[qtmp_res[rot2]], writes=[R(O_SCR, 512)], tag=f"eluexp{q}_{i}")

        def vf(v, q=q, i=i, rot2=rot2):
            v.tensor_scalar(out=e2, in0=escr, scalar1=1.0, scalar2=1.0, op0=ALU.min, op1=ALU.subtract)
            return v.scalar_tensor_tensor(out=qh[:, i, q * 512:(q + 1) * 512], in0=qtmp[rot2],
                                          scalar=0.0, in1=e2, op0=ALU.max, op1=ALU.add)
        P.step("vector", vf, reads=[R(O_SCR, 512), R(O_SCR + 512, 512), qtmp_res[rot2]],
               writes=[R(O_QH + i * 1024 + q * 256, 256), R(O_SCR + 512, 512)], tag=f"eluf{q}_{i}")

    wpass("wqk", wqk, 0, evac_qk, None)

    def evac_v(q, i, b, psr):
        rot2 = i % 2
        P.step("vector", lambda v, b=b, q=q, rot2=rot2: v.tensor_add(
            qtmp[rot2], pA[:, b * 512:(b + 1) * 512], brep_full[:, q * 512:(q + 1) * 512]),
            reads=[psr, row_res], writes=[qtmp_res[rot2]], tag=f"baddv{q}_{i}")
        P.step("scalar", lambda s, q=q, i=i, rot2=rot2: s.activation(
            out=vh[:, i, q * 512:(q + 1) * 512], in_=qtmp[rot2], func=AF.Silu),
            reads=[qtmp_res[rot2]], writes=[R(O_VOW + i * 1024 + q * 256, 256)], tag=f"silv{q}_{i}")

    wpass("wv", wv, 1, evac_v, None)

    slot_list = [(pA, 0), (pA, 512), (pA, 1024), (pA, 1536), (pB, 0), (pB, 512), (pB, 1024)]
    heads = list(range(16))
    hi = 0
    while hi < 16:
        rnd = heads[hi:hi + 7]
        for k, h in enumerate(rnd):
            t_, off = slot_list[k]
            psr = PS(t_, off, 128)

            def kf(t, t_=t_, off=off, h=h):
                last = None
                for i in range(TILES):
                    last = t.matmul(t_[:, off:off + 128], qh[:, i, h * 128:(h + 1) * 128],
                                    vh[:, i, h * 128:(h + 1) * 128], start=(i == 0), stop=(i == TILES - 1))
                return last
            P.step("tensor", kf, reads=[R(O_QH, 8192), R(O_VOW, 8192)], writes=[psr], tag=f"kv{h}")
        for k, h in enumerate(rnd):
            t_, off = slot_list[k]
            psr = PS(t_, off, 128)
            P.step("vector", lambda v, t_=t_, off=off, h=h: v.tensor_copy(kv_part[:, h, :], t_[:, off:off + 128]),
                   reads=[psr], writes=[R(O_KVP + h * 128, 128)], tag=f"kvc{h}")
        hi += 7
    P.step("sync", lambda e: e.dma_start(out=cc_in.rearrange("(h d) e -> d h e", d=128), in_=kv_part),
           reads=[R(O_KVP, 2048)], writes=[("cc_in", 0, 1)], dma=True, tag="cc_w")
    cc_sem = es.enter_context(nc.semaphore("cc_sem"))

    def emit_collective():
        def ccf(e):
            e.collective_compute(
                "AllReduce", ALU.add, replica_groups=[[0, 1, 2, 3], [4, 5, 6, 7]],
                ins=[cc_in.opt()], outs=[cc_out.opt()]).then_inc(cc_sem)
            e.wait_ge(cc_sem, 1)
            return e.nop()
        P.step("gpsimd", ccf, reads=[("cc_in", 0, 1)], writes=[("cc_out", 0, 1)], tag="ccar")
        P.step("gpsimd", lambda e: e.dma_start(out=kv_bf,
                                               in_=cc_out.rearrange("(h d) e -> d h e", d=128)),
               reads=[("cc_out", 0, 1)], writes=[R(O_KVB, 1024)], dma=True, tag="cc_r")
        if "kvb" in dbg:
            P.step("sync", lambda e: e.dma_start(out=dbg["kvb"], in_=kv_bf.rearrange("p h e -> p (h e)")),
                   reads=[R(O_KVB, 1024)], dma=True)

    def evac_u(q, i, b, psr):
        rot2 = i % 2
        P.step("vector", lambda v, b=b, q=q, rot2=rot2: v.tensor_add(
            qtmp[rot2], pA[:, b * 512:(b + 1) * 512], brep_full[:, q * 512:(q + 1) * 512]),
            reads=[psr, row_res], writes=[qtmp_res[rot2]], tag=f"baddu{q}_{i}")
        P.step("scalar", lambda s, i=i, rot2=rot2: s.activation(
            out=ubuf[:, i, :], in_=qtmp[rot2], func=AF.Silu),
            reads=[qtmp_res[rot2]], writes=[R(O_UB + i * 256, 256)], tag=f"silu{q}_{i}")
        if i == TILES - 1:
            P.step("sync", lambda e, q=q: e.dma_start(
                out=u_dram[:, :, q, :].rearrange("t p n -> p t n"), in_=ubuf[:]),
                reads=[R(O_UB, 2048)], writes=[("u_dram", q, q + 1)], dma=True, tag=f"udma{q}")
            if "ub" in dbg:
                P.step("sync", lambda e, q=q: e.dma_start(
                    out=dbg["ub"][:, q * 4096:(q + 1) * 4096],
                    in_=ubuf.rearrange("p t n -> p (t n)")),
                    reads=[R(O_UB, 2048)], dma=True)

    global _CC_HOOK
    _CC_HOOK = emit_collective
    wpass("wu", wu, 2, evac_u, None)
    _CC_HOOK = None

    drip_abs(200)

    if "qh" in dbg:
        P.step("sync", lambda e: e.dma_start(out=dbg["qh"], in_=qh.rearrange("p t f -> p (t f)")),
               reads=[R(O_QH, 8192)], dma=True)
    if "vh" in dbg:
        P.step("sync", lambda e: e.dma_start(out=dbg["vh"], in_=vh.rearrange("p t f -> p (t f)")),
               reads=[R(O_VOW, 8192)], dma=True)
    if "kv" in dbg:
        P.step("sync", lambda e: e.dma_start(out=dbg["kv"], in_=kv_part.rearrange("p h e -> p (h e)")),
               reads=[R(O_KVP, 2048)], dma=True)

    for j, (dst, dstres) in enumerate([(grep, grep_res), (brep, brep_res)]):
        P.step("sync", lambda e, j=j: e.dma_start(out=row, in_=vecs[4 + j:5 + j, :]),
               reads=[("vecs", 0, 1)], writes=[row_res], dma=True, tag=f"gbrow{j}")

        def gf(t, j=j):
            last = None
            for q in range(4):
                last = t.matmul(pA[:, q * 512:(q + 1) * 512], ones_sb[0:1, :],
                                row[0:1, q * 512:(q + 1) * 512], start=True, stop=True)
            return last
        P.step("tensor", gf, reads=[row_res, ONES], writes=[PS(pA, 0, 2048)], tag=f"gbmm{j}")
        P.step("vector", lambda v, dst=dst: v.tensor_copy(dst, pA[:]),
               reads=[PS(pA, 0, 2048)], writes=[dstres], tag=f"gbc{j}")

    if "ud" in dbg:
        P.step("gpsimd", lambda e: e.dma_start(out=dbg["ud"], in_=u_dram.rearrange("t p q n -> (t p q) n")),
               reads=[("u_dram", 0, 4)], dma=True)
    if "gb" in dbg:
        P.step("sync", lambda e: e.dma_start(out=dbg["gb"][:, 0:D], in_=grep), reads=[grep_res], dma=True)
        P.step("sync", lambda e: e.dma_start(out=dbg["gb"][:, D:2 * D], in_=brep), reads=[brep_res], dma=True)
    for i in range(TILES):
        rot = i % 2
        P.step("gpsimd", lambda e, rot=rot, i=i: e.dma_start(
            out=urel[rot], in_=u_dram[i, :, :, :]),
            reads=[("u_dram", 0, 4)], writes=[urel_res[rot]], dma=True, tag=f"urel{i}")
        for half in range(2):
            ps, psres = (pT, PT0) if half == 0 else (pT1, PT1)
            transpose_batch(lambda k, i=i, half=half: qh[:, i, (half * 8 + k) * 128:(half * 8 + k + 1) * 128],
                            8, ps, psres, [R(O_QH + i * 1024, 1024)], tag=f"qt{i}_{half}")
            P.step("vector", lambda v, ps=ps, rot=rot, half=half: v.tensor_copy(
                qfm[rot][:, half * 8:(half + 1) * 8, :],
                ps[:, 0:1024].rearrange("p (c n) -> p c n", c=8)),
                reads=[psres], writes=[qfm_res[rot]], tag=f"qtc{i}_{half}")
        if "u" in dbg:
            P.step("sync", lambda e, rot=rot, i=i: e.dma_start(
                out=dbg["u"][:, i * D:(i + 1) * D], in_=urel[rot].rearrange("p q n -> p (q n)")),
                reads=[urel_res[rot]], dma=True)
        def of(t, rot=rot):
            last = None
            for h in range(16):
                last = t.matmul(pA[:, h * 128:(h + 1) * 128], qfm[rot][:, h, :], kv_bf[:, h, :],
                                start=(h % 4 == 0), stop=(h % 4 == 3))
            return last
        P.step("tensor", of, reads=[qfm_res[rot], R(O_KVB, 1024)], writes=[PS(pA, 0, 2048)], tag=f"omm{i}")
        osb_i = x_rot[rot]
        osb_res = xr_res[rot]
        P.step("scalar", lambda s, osb_i=osb_i: s.copy(out=osb_i, in_=pA[:]),
               reads=[PS(pA, 0, 2048)], writes=[osb_res], tag=f"oevac{i}")
        if "o" in dbg:
            P.step("sync", lambda e, i=i, osb_i=osb_i: e.dma_start(out=dbg["o"][:, i * D:(i + 1) * D], in_=osb_i),
                   reads=[osb_res], dma=True)
        ln_stats(rot, osb_i, osb_res, c_eps[:], EPS_RES, tag=f"ln1_{i}")
        ln_apply(rot, osb_i, osb_res, osb_i, osb_res, tag=f"ln1_{i}")
        if "oln" in dbg:
            P.step("sync", lambda e, i=i, osb_i=osb_i: e.dma_start(out=dbg["oln"][:, i * D:(i + 1) * D], in_=osb_i),
                   reads=[osb_res], dma=True)
        owb_i = ab[:, 2 * (O_SCR + rot * 1024):2 * (O_SCR + rot * 1024) + 2048]
        owb_res = R(O_SCR + rot * 1024, 1024)

        P.step("vector", lambda v, osb_i=osb_i: v.tensor_mul(osb_i, osb_i, grep),
               reads=[osb_res, grep_res], writes=[osb_res], tag=f"gmul{i}")
        P.step("gpsimd", lambda g, osb_i=osb_i: g.tensor_add(osb_i, osb_i, brep),
               reads=[osb_res, brep_res], writes=[osb_res], tag=f"badd3_{i}")
        P.step("vector", lambda v, rot=rot, osb_i=osb_i, owb_i=owb_i: v.tensor_mul(
            owb_i, osb_i, urel[rot].rearrange("p q n -> p (q n)")),
            reads=[osb_res, urel_res[rot]], writes=[owb_res], tag=f"umul{i}")
        if "ow" in dbg:
            P.step("sync", lambda e, i=i, owb_i=owb_i: e.dma_start(out=dbg["ow"][:, i * D:(i + 1) * D],
                                                                   in_=owb_i),
                   reads=[owb_res], dma=True)
        for half in range(2):
            ps, psres = (pT, PT0) if half == 0 else (pT1, PT1)
            transpose_batch(lambda k, half=half, rot=rot: ab[:, 2 * (O_SCR + rot * 1024) + (half * 8 + k) * 128:
                                                             2 * (O_SCR + rot * 1024) + (half * 8 + k + 1) * 128],
                            8, ps, psres, [owb_res], tag=f"owt{i}_{half}")
            P.step("vector", lambda v, ps=ps, i=i, half=half: v.tensor_copy(
                owfm[:, half * 8:(half + 1) * 8, i, :],
                ps[:, 0:1024].rearrange("p (c n) -> p c n", c=8)),
                reads=[psres], writes=[R(O_VOW + (half * 8) * 512, 4096)], tag=f"owc{i}_{half}")
        drip_abs(4)

    repl_bias("wo", 3)
    wor = wo.rearrange("(c p) n -> p c n", p=128)
    for q in range(4):
        rot = q % 2
        P.step("gpsimd", lambda e, rot=rot, q=q: e.dma_start(
            out=wbuf[rot][:], in_=wor[:, :, q * 512:(q + 1) * 512]),
            reads=[("wo", q, q + 1)], writes=[wbuf_res[rot]], dma=True, tag=f"wodma{q}")
        for i in range(TILES):
            b = (q * TILES + i) % 4
            psr = PS(pA, b * 512, 512)

            def mo(t, rot=rot, q=q, i=i, b=b):
                last = None
                for fc in range(DC):
                    last = t.matmul(pA[:, b * 512:(b + 1) * 512], owfm[:, fc, i, :], wbuf[rot][:, fc, :],
                                    start=(fc == 0), stop=(fc == DC - 1))
                return last
            P.step("tensor", mo, reads=[wbuf_res[rot], R(O_VOW, 8192)],
                   writes=[psr], tag=f"womm{q}_{i}")
            P.step("vector", lambda v, b=b, q=q, i=i: v.tensor_add(
                a_sb[i][:, q * 512:(q + 1) * 512], pA[:, b * 512:(b + 1) * 512],
                brep_full[:, q * 512:(q + 1) * 512]),
                reads=[psr, row_res], writes=[(a_res[i][0], a_res[i][1] + q * 512, a_res[i][1] + (q + 1) * 512)],
                tag=f"woev{q}_{i}")
        drip_abs(2)

    if "asb" in dbg:
        for i in range(TILES):
            P.step("sync", lambda e, i=i: e.dma_start(out=dbg["asb"][:, i * 2048:(i + 1) * 2048], in_=a_sb[i]),
                   reads=[a_res[i]], dma=True)

    for i in range(TILES):
        rot = i % 2
        P.step("sync", lambda e, i=i: e.dma_start(out=x1rel, in_=x[i * 128:(i + 1) * 128, :]),
               reads=[("x", i, i + 1)], writes=[R(O_X1, 2048)], dma=True, tag=f"xrel{i}")
        ln_stats(rot, a_sb[i], a_res[i], c_eps[:], EPS_RES, tag=f"lna_{i}")
        ln_apply(rot, a_sb[i], a_res[i], scr, scr_res, tag=f"lna_{i}")
        P.step("vector", lambda v: v.tensor_add(x1rel, scr, x1rel),
               reads=[scr_res, R(O_X1, 2048)], writes=[R(O_X1, 2048)], tag=f"x1add{i}")
        P.step("sync", lambda e, i=i: e.dma_start(out=x1_dram[i, :, :], in_=x1rel),
               reads=[R(O_X1, 2048)], writes=[("x1d", i, i + 1)], dma=True, tag=f"x1w{i}")
        ln_stats(rot, x1rel, R(O_X1, 2048), c_eps[:], EPS_RES, tag=f"ln2_{i}")
        ln_apply(rot, x1rel, R(O_X1, 2048), ln2st[rot], qfm_res[rot], tag=f"ln2_{i}")
        for half in range(2):
            ps, psres = (pT, PT0) if half == 0 else (pT1, PT1)
            transpose_batch(lambda k, rot=rot, half=half: ln2st[rot][:, (half * 8 + k) * 128:(half * 8 + k + 1) * 128],
                            8, ps, psres, [qfm_res[rot]], tag=f"ln2t{i}_{half}")
            P.step("vector", lambda v, ps=ps, i=i, half=half: v.tensor_copy(
                act_fm[:, half * 8:(half + 1) * 8, i, :],
                ps[:, 0:1024].rearrange("p (c n) -> p c n", c=8)),
                reads=[psres], writes=[R(O_ACT + (half * 8) * 512, 4096)], tag=f"ln2c{i}_{half}")
        drip_abs(2)

    if "x1" in dbg:
        for i in range(TILES):
            P.step("gpsimd", lambda e, i=i: e.dma_start(out=dbg["x1"][i * 128:(i + 1) * 128, :], in_=x1_dram[i, :, :]),
                   reads=[("x1d", i, i + 1)], dma=True)

    drip_abs(200)

    P.barrier("barB2C")

    def redsum(v):
        v.tensor_reduce(out=smalls[:, 0:1], in_=wcols[:], axis=AX.X, op=ALU.add)
        v.drain()
        return v.tensor_reduce(out=smalls[:, 1:2], in_=pcols[:], axis=AX.X, op=ALU.add)
    P.step("vector", redsum, reads=[SM("wcols"), SM("pcols")], writes=[SM("sums")], tag="redsum")

    def totmm(t):
        t.matmul(pB[:, 1024:1025], ones_sb[:], smalls[:, 0:1], start=True, stop=True)
        return t.matmul(pB[:, 1028:1029], ones_sb[:], smalls[:, 1:2], start=True, stop=True)
    P.step("tensor", totmm, reads=[SM("sums"), ONES], writes=[PS(pB, 1024, 512)], tag="totmm")

    INV_N = float(np.float32(1.0 / NW))

    def chain(v):
        d = v.drain
        v.tensor_scalar(out=smalls[:, 2:3], in0=pB[:, 1024:1025], scalar1=INV_N, scalar2=EPS,
                        op0=ALU.mult, op1=ALU.max); d()
        v.tensor_scalar(out=smalls[:, 3:4], in0=pB[:, 1028:1029], scalar1=INV_N, scalar2=EPS,
                        op0=ALU.mult, op1=ALU.max); d()
        v.tensor_scalar(out=thr[:, 0:1], in0=smalls[:, 2:3], scalar1=0.7, scalar2=None, op0=ALU.mult); d()
        v.tensor_scalar(out=thr[:, 1:2], in0=thr[:, 0:1], scalar1=-1.0, scalar2=None, op0=ALU.mult); d()
        v.tensor_scalar(out=thr[:, 2:3], in0=smalls[:, 3:4], scalar1=0.7, scalar2=None, op0=ALU.mult); d()
        v.tensor_scalar(out=thr[:, 3:4], in0=thr[:, 2:3], scalar1=-1.0, scalar2=None, op0=ALU.mult); d()
        v.tensor_mul(smalls[:, 4:5], smalls[:, 2:3], smalls[:, 2:3]); d()
        v.tensor_mul(smalls[:, 5:6], smalls[:, 4:5], smalls[:, 3:4]); d()
        v.tensor_scalar(out=smalls[:, 6:7], in0=smalls[:, 5:6], scalar1=0.125, scalar2=None, op0=ALU.mult); d()
        v.reciprocal(smalls[:, 7:8], smalls[:, 6:7]); d()
        v.tensor_mul(smalls[:, 0:1], smalls[:, 7:8], smalls[:, 7:8]); d()
        v.tensor_scalar(out=epsc[:], in0=smalls[:, 0:1], scalar1=EPS, scalar2=None, op0=ALU.mult); d()
        return v.tensor_copy(smalls[:, 1:2], epsc[:])
    P.step("vector", chain, reads=[PS(pB, 1024, 512)], writes=[SM("thr"), SM("epsc"), SM("sums")], tag="chain")

    if "sc" in dbg:
        P.step("sync", lambda e: e.dma_start(out=dbg["sc"][:, 0:8], in_=smalls[:]),
               reads=[SM("thr"), SM("epsc"), SM("sums")], dma=True, tag="scdump1")
        P.step("sync", lambda e: e.dma_start(out=dbg["sc"][:, 8:12], in_=thr[:]),
               reads=[SM("thr")], dma=True, tag="scdump2")
    if "cols" in dbg:
        P.step("sync", lambda e: e.dma_start(out=dbg["cols"][:, 0:80], in_=wcols[:]),
               reads=[SM("wcols")], dma=True, tag="coldump1")
        P.step("sync", lambda e: e.dma_start(out=dbg["cols"][:, 80:160], in_=pcols[:]),
               reads=[SM("pcols")], dma=True, tag="coldump2")

    wfc_rows = wfc.rearrange("(a p) d -> p a d", p=128)
    wpj_cols = wpj.rearrange("(c p) h -> p c h", p=128)

    groups = [list(range(g, min(g + GRP, HCN))) for g in range(0, HCN, GRP)]
    for gi, grp in enumerate(groups):
        for sj, hc in enumerate(grp):
            rot = hc % 2
            P.step("sync", lambda e, rot=rot, hc=hc: e.dma_start(out=wfcb[rot], in_=wfc_rows[:, hc, :]),
                   reads=[("wfc", hc, hc + 1)], writes=[wfcb_res[rot]], dma=True, tag=f"fcd{hc}")
            P.step("sync", lambda e, rot=rot, hc=hc: e.dma_start(
                out=wpb[rot], in_=wpj_cols[:, :, hc * 128:(hc + 1) * 128]),
                reads=[("wpj", hc, hc + 1)], writes=[wpb_res[rot]], dma=True, tag=f"wpd{hc}")
            def tf1(s, rot=rot):
                s.activation(out=s1, in_=wfcb[rot], func=AF.Sign, bias=thr[:, 1:2])
                return s.activation(out=s2, in_=wfcb[rot], func=AF.Sign, bias=thr[:, 0:1])
            P.step("scalar", tf1, reads=[wfcb_res[rot], SM("thr")], writes=[s1_res, s2_res], tag=f"sgf{hc}")
            P.step("vector", lambda v: v.tensor_add(s1, s1, s2),
                   reads=[s1_res, s2_res], writes=[s1_res], tag=f"qf{hc}")
            def tf2(s, rot=rot):
                s.activation(out=q2wp_flat, in_=wpb_flat[rot], func=AF.Sign, bias=thr[:, 3:4])
                return s.activation(out=s2, in_=wpb_flat[rot], func=AF.Sign, bias=thr[:, 2:3])
            P.step("scalar", tf2, reads=[wpb_res[rot], SM("thr")], writes=[q2wp_res, s2_res], tag=f"sgp{hc}")
            P.step("vector", lambda v: v.tensor_add(q2wp_flat, q2wp_flat, s2),
                   reads=[q2wp_res, s2_res], writes=[q2wp_res], tag=f"qp{hc}")
            wfT = wfT2_list[hc % 2]
            wfT_res = wfT2_res[hc % 2]
            for half in range(2):
                ps, psres = (pT, PT0) if half == 0 else (pT1, PT1)
                transpose_batch(lambda k, half=half: s1[:, (half * 8 + k) * 128:(half * 8 + k + 1) * 128],
                                8, ps, psres, [s1_res], tag=f"ft{hc}_{half}")
                P.step("vector", lambda v, ps=ps, half=half, wfT=wfT: v.tensor_copy(
                    wfT[:, half * 8:(half + 1) * 8, :],
                    ps[:, 0:1024].rearrange("p (c n) -> p c n", c=8)),
                    reads=[psres], writes=[wfT_res], tag=f"ftc{hc}_{half}")
            for half in range(2):
                ps, psres = (pT, PT0) if half == 0 else (pT1, PT1)
                transpose_batch(lambda k, half=half: q2wp[:, half * 8 + k, :],
                                8, ps, psres, [q2wp_res], tag=f"pt{hc}_{half}")
                P.step("vector", lambda v, ps=ps, sj=sj, half=half: v.tensor_copy(
                    wpT_s(sj)[:, half * 1024:(half + 1) * 1024], ps[:, 0:1024]),
                    reads=[psres], writes=[wpT_r(sj)], tag=f"ptc{hc}_{half}")
            for half in range(2):
                psr = PS(pB, half * 512, 512)

                def rf(t, half=half, wfT=wfT):
                    last = None
                    for dc in range(DC):
                        last = t.matmul(pB[:, half * 512:(half + 1) * 512], wfT[:, dc, :],
                                        act_fm[:, dc, half * 4:(half + 1) * 4, :].rearrange("p t n -> p (t n)"),
                                        start=(dc == 0), stop=(dc == DC - 1))
                    return last
                P.step("tensor", rf, reads=[wfT_res, R(O_ACT, 8192)], writes=[psr], tag=f"rmm{hc}_{half}")
                P.step("vector", lambda v, half=half: v.tensor_scalar(
                    out=scr[:, half * 512:(half + 1) * 512], in0=pB[:, half * 512:(half + 1) * 512],
                    scalar1=0.0, scalar2=None, op0=ALU.max),
                    reads=[psr], writes=[(("ar", O_SCR + half * 512, O_SCR + (half + 1) * 512))], tag=f"rr{hc}_{half}")
                P.step("vector", lambda v, sj=sj, half=half: v.tensor_mul(
                    r_s(sj)[:, half * 512:(half + 1) * 512], scr[:, half * 512:(half + 1) * 512],
                    scr[:, half * 512:(half + 1) * 512]),
                    reads=[("ar", O_SCR + half * 512, O_SCR + (half + 1) * 512), psr],
                    writes=[r_r(sj)], tag=f"rsq{hc}_{half}")

        for i in range(TILES):
            for cg in range(4):
                psr = PS(pA, cg * 512, 512)

                def ff(t, i=i, cg=cg, grp=grp):
                    last = None
                    for sj in range(len(grp)):
                        last = t.matmul(pA[:, cg * 512:(cg + 1) * 512], r_s(sj)[:, i * 128:(i + 1) * 128],
                                        wpT_s(sj)[:, cg * 512:(cg + 1) * 512],
                                        start=(sj == 0), stop=(sj == len(grp) - 1))
                    return last
                P.step("tensor", ff, reads=r_all[:len(grp)] + wpT_all[:len(grp)],
                       writes=[psr], tag=f"fmm{gi}_{i}_{cg}")
            if gi == 0:
                P.step("vector", lambda v, i=i: v.tensor_copy(f_sb[i], pA[:]),
                       reads=[PS(pA, 0, 2048)], writes=[f_res[i]], tag=f"fc{gi}_{i}")
            else:
                P.step("vector", lambda v, i=i: v.tensor_add(f_sb[i], f_sb[i], pA[:]),
                       reads=[PS(pA, 0, 2048), f_res[i]], writes=[f_res[i]], tag=f"fa{gi}_{i}")

    if "fsb" in dbg:
        for i in range(TILES):
            P.step("sync", lambda e, i=i: e.dma_start(out=dbg["fsb"][:, i * 2048:(i + 1) * 2048], in_=f_sb[i]),
                   reads=[f_res[i]], dma=True)

    for i in range(TILES):
        rot = i % 2
        P.step("gpsimd", lambda e, i=i: e.dma_start(out=x1rel, in_=x1_dram[i, :, :]),
               reads=[("x1d", i, i + 1)], writes=[R(O_X1, 2048)], dma=True, tag=f"x1r{i}")
        ln_stats(rot, f_sb[i], f_res[i], epsc[:], SM("epsc"), tag=f"lnf_{i}")
        ln_apply(rot, f_sb[i], f_res[i], scr, scr_res, tag=f"lnf_{i}")
        P.step("vector", lambda v: v.tensor_add(scr, scr, x1rel),
               reads=[scr_res, R(O_X1, 2048)], writes=[scr_res], tag=f"oadd{i}")
        P.step("sync", lambda e, i=i: e.dma_start(out=y[i * 128:(i + 1) * 128, :], in_=scr),
               reads=[scr_res], writes=[("y", i, i + 1)], dma=True, tag=f"yw{i}")

    P.emit(es)
    es.close()
    return nc


_NC_CACHE = {}


def _get_nc(debug=()):
    key = tuple(sorted(debug))
    if key not in _NC_CACHE:
        _NC_CACHE[key] = build(debug=key)
    return _NC_CACHE[key]


def kernel(L0_attn_x, L0_attn_Wqk, L0_attn_bqk, L0_attn_Wv, L0_attn_bv,
           L0_attn_Wu, L0_attn_bu, L0_attn_Wo, L0_attn_bo,
           L0_attn_ln_g, L0_attn_ln_b, L0_ffn_w_fc, L0_ffn_w_proj,
           _debug=(), _results_hook=None):
    f32 = np.float32
    xfull = np.ascontiguousarray(np.asarray(L0_attn_x, f32).reshape(8 * NT, D))
    vecs = np.ascontiguousarray(np.stack([
        np.asarray(L0_attn_bqk, f32), np.asarray(L0_attn_bv, f32),
        np.asarray(L0_attn_bu, f32), np.asarray(L0_attn_bo, f32),
        np.asarray(L0_attn_ln_g, f32), np.asarray(L0_attn_ln_b, f32)], axis=0))
    base = dict(
        wqk=np.ascontiguousarray(np.asarray(L0_attn_Wqk, f32)),
        wv=np.ascontiguousarray(np.asarray(L0_attn_Wv, f32)),
        wu=np.ascontiguousarray(np.asarray(L0_attn_Wu, f32)),
        wo=np.ascontiguousarray(np.asarray(L0_attn_Wo, f32)),
        vecs=vecs,
        wfc=np.ascontiguousarray(np.asarray(L0_ffn_w_fc, f32)),
        wpj=np.ascontiguousarray(np.asarray(L0_ffn_w_proj, f32)),
        ident=np.eye(128, dtype=BF),
        ones=np.ones((128, 128), dtype=f32),
    )
    nc = _get_nc(_debug)
    in_maps = [dict(base, x=np.ascontiguousarray(xfull[i * NT:(i + 1) * NT])) for i in range(8)]
    res = run_bass_kernel_spmd(nc, in_maps, core_ids=list(range(8)))
    if _results_hook is not None:
        _results_hook(res)
    out = np.concatenate([np.asarray(res.results[i]["y"]) for i in range(8)], axis=0)
    return out.reshape(2, 4096, D).astype(np.float32)
